# version 06
# baseline (speedup 1.0000x reference)
import numpy as np
from contextlib import ExitStack

import concourse.bass as bass
import concourse.mybir as mybir
from concourse.bass_utils import run_bass_kernel_spmd

F32 = mybir.dt.float32
BF16 = mybir.dt.bfloat16
AF = mybir.ActivationFunctionType
ALU = mybir.AluOpType
AXX = mybir.AxisListType.X

D = 1024
S = 2048
B = 8
NT = S // 128
ND = D // 128
H_FF0 = 2816
NH0 = H_FF0 // 128
E0 = 8
EPS_LN = 1e-5
EPS_RMS = 1e-6
PI = float(np.pi)


class Prog:

    ENGS = ("tensor", "vector", "scalar", "gpsimd", "sync")

    PSUM_BANKS = {
        "PA": (0, 1, 2, 3), "PAg": (0,), "PAu": (1,), "PAe0": (2,),
        "PAe1": (3,), "PAg1": (0, 1), "PAu1": (2, 3),
        "paB0": (0,), "paB1": (1,), "paB2": (2,), "paB3": (3,),
        "PB": (4, 5), "PBg": (4,), "PBu": (5,),
        "PC": (6, 7), "pcv0": (6,), "pcv1024": (7,),
        "sp2": (0, 1, 2, 3), "delta": (0, 1, 2, 3), "xc16": (0, 1, 2, 3),
        "hscan": (0, 1, 2, 3),
    }

    @classmethod
    def _xkeys(cls, keys):
        out = []
        for k in keys:
            b = cls.PSUM_BANKS.get(k)
            if b is None:
                out.append(k)
            else:
                out.extend(f"psb{i}" for i in b)
        return tuple(out)

    def __init__(self, nc, st):
        self.nc = nc
        self.sems = {e: st.enter_context(nc.semaphore(f"c_{e}")) for e in self.ENGS}
        self.dsems = {q: st.enter_context(nc.semaphore(f"d_{q}"))
                      for q in ("sync", "gpsimd", "scalar")}
        self.items = {e: [] for e in self.ENGS}
        self.inc_count = {e: 0 for e in self.ENGS}
        self.dma_count = {q: 0 for q in self.dsems}
        self.pending = {e: [] for e in self.ENGS}
        self.last_write = {}
        self.readers = {}
        self.synced = {e: {} for e in self.ENGS}
        self.nops = 0

    class _Pend:
        __slots__ = ("val", "who")
        def __init__(self):
            self.val = None
            self.who = "?"

    def _resolve(self, k, v, eng, pe_chain):
        if isinstance(v, Prog._Pend):
            if v.val is None:
                if k == ("c", eng) and eng == "tensor" and pe_chain:
                    return None
                raise RuntimeError(f"{eng} op depends on un-milestoned {k} op")
            return v.val
        return v

    def _gather(self, eng, reads, writes, pe_chain=False):
        need = {}
        def add(marks):
            for k, v in marks.items():
                if pe_chain and eng == "tensor" and k == ("c", "tensor"):
                    continue
                val = self._resolve(k, v, eng, pe_chain)
                if val is None:
                    continue
                if need.get(k, -1) < val:
                    need[k] = val
        for r in reads:
            add(self.last_write.get(r, {}))
        for w in writes:
            add(self.last_write.get(w, {}))
            add(self.readers.get(w, {}))
        waits = []
        for k, v in sorted(need.items(), key=str):
            if self.synced[eng].get(k, -1) >= v:
                continue
            waits.append(((self.sems if k[0] == "c" else self.dsems)[k[1]], v))
            self.synced[eng][k] = v
        return waits

    def _record_rw(self, key, mark, reads, writes):
        for w in writes:
            self.last_write[w] = {key: mark}
            self.readers[w] = {}
        for r in reads:
            self.readers.setdefault(r, {})[key] = mark

    def op(self, eng, builder, reads=(), writes=(), inc=True, pe_chain=False):
        reads = self._xkeys(reads); writes = self._xkeys(writes)
        waits = self._gather(eng, reads, writes, pe_chain)
        key = ("c", eng)
        if inc:
            self.inc_count[eng] += 1
            val = self.inc_count[eng]
            for p in self.pending[eng]:
                p.val = val
            self.pending[eng] = []
            mark = val
            incs = [(self.sems[eng], 1)]
        else:
            mark = Prog._Pend()
            self.pending[eng].append(mark)
            incs = []
        self._record_rw(key, mark, reads, writes)
        self.items[eng].append((waits, builder, incs))

    def dma(self, q, out_ap, in_ap, reads=(), writes=(), **kw):
        reads = self._xkeys(reads); writes = self._xkeys(writes)
        waits = self._gather(q, reads, writes)
        self.dma_count[q] += 1
        key = ("d", q)
        mark = 16 * self.dma_count[q]
        self._record_rw(key, mark, reads, writes)
        def builder(eng, out_ap=out_ap, in_ap=in_ap, kw=kw):
            return eng.dma_start(out=out_ap, in_=in_ap, **kw)
        self.items[q].append((waits, builder, [(self.dsems[q], 16)]))

    def barrier(self):
        for e in self.ENGS:
            assert not self.pending[e], f"barrier with pending marks on {e}"
        marks = [(("c", e), self.inc_count[e]) for e in self.ENGS
                 if self.inc_count[e] > 0]
        marks += [(("d", q), 16 * self.dma_count[q]) for q in self.dsems
                  if self.dma_count[q] > 0]
        for eng in self.ENGS:
            waits = []
            for k, v in marks:
                if k == ("c", eng):
                    continue
                if self.synced[eng].get(k, -1) >= v:
                    continue
                waits.append(((self.sems if k[0] == "c" else self.dsems)[k[1]], v))
                self.synced[eng][k] = v
            if waits:
                self.items[eng].append((waits, None, []))
        self.last_write.clear()
        self.readers.clear()

    def fork(self, src_key, dst_keys):
        m = self.last_write.get(src_key, {})
        r = self.readers.get(src_key, {})
        for d in dst_keys:
            self.last_write[d] = dict(m)
            self.readers[d] = dict(r)

    def join(self, dst_key, src_keys):
        lw, rd = {}, {}
        def mval(v):
            return v.val if isinstance(v, Prog._Pend) else v
        for s in src_keys:
            for k, v in self.last_write.get(s, {}).items():
                if k not in lw or (mval(lw[k]) or -1) < (mval(v) or 1 << 60):
                    lw[k] = v
            for k, v in self.readers.get(s, {}).items():
                if k not in rd or (mval(rd[k]) or -1) < (mval(v) or 1 << 60):
                    rd[k] = v
        self.last_write[dst_key] = lw
        self.readers[dst_key] = rd

    def finalize(self, block):
        for q, cnt in self.dma_count.items():
            if cnt:
                self.items[q].append(([(self.dsems[q], 16 * cnt)], None, []))
        def make_body(eng_name):
            def body(eng):
                for waits, builder, incs in self.items[eng_name]:
                    for sem, val in waits:
                        eng.wait_ge(sem, val)
                    if builder is not None:
                        inst = builder(eng)
                        for sem, dv in incs:
                            inst.then_inc(sem, dv)
            return body
        block.tensor(make_body("tensor"))
        block.vector(make_body("vector"))
        block.scalar(make_body("scalar"))
        block.gpsimd(make_body("gpsimd"))
        block.sync(make_body("sync"))


_DBG = {}

def build_program(upto=4, taps=()):
    nc = bass.Bass("TRN2", target_bir_lowering=False)
    taps = set(taps)

    dram = {}
    def din(name, shape):
        dram[name] = nc.dram_tensor(name, list(shape), F32, kind="ExternalInput").ap()
        return dram[name]

    din("xr", (B, S, D))
    if upto >= 2:
        din("l0_gate_wT", (D, E0))
        din("l0_gate_b", (E0, 1))
        din("l0_wgT", (E0, D, H_FF0))
        din("l0_wuT", (E0, D, H_FF0))
        din("l0_wdT", (E0, H_FF0, D))
    if upto >= 3:
        din("l1_winT", (D, 2064))
        din("l1_bin", (2064, 1))
        din("l1_wconv", (2048, 4))
        din("l1_bconv", (2048, 1))
        din("l1_wx96T", (2048, 96))
        din("l1_bdt", (16, 1))
        din("l1_wout16T", (16, D))
    if upto >= 4:
        din("l1f_g1", (D, 1))
        din("l1f_g2", (D, 1))
        din("l1f_WqT", (D, D))
        din("l1f_WkcT", (D, 256))
        din("l1f_WvcT", (D, 256))
        din("l1f_WoT", (D, D))
        din("l1f_WrT", (D, E0))
        din("l1f_ebias", (E0, 1))
        din("l1f_sW1T", (D, D))
        din("l1f_sW2T", (D, D))
        din("l1f_sW3T", (D, D))
        din("l1f_eW1T", (E0, D, D))
        din("l1f_eW2T", (E0, D, D))
        din("l1f_eW3T", (E0, D, D))

    out = nc.dram_tensor("out", [S, D], F32, kind="ExternalOutput").ap()
    if upto >= 4:
        dram["x3scr"] = nc.dram_tensor("x3scr", [128, ND * S], F32,
                                       kind="Internal").ap()
    tap_aps = {t: nc.dram_tensor("tap_" + t, [S, D], F32, kind="ExternalOutput").ap()
               for t in taps}
    for name in _DBG:
        tap_aps["dbg_" + name] = nc.dram_tensor(
            "dbg_" + name, [128, 2048], F32, kind="ExternalOutput").ap()

    st = ExitStack()
    sbufs = {}
    def sb(name, shape, dtype):
        sbufs[name] = st.enter_context(nc.sbuf_tensor(name, list(shape), dtype))
        return sbufs[name]
    def psum(name, shape, dtype):
        sbufs[name] = st.enter_context(nc.psum_tensor(name, list(shape), dtype))
        return sbufs[name]

    with st:
        sb("RES", [128, ND * S], F32)
        sb("FB", [128, 2 * ND * S], BF16)
        sb("BFX", [128, ND * S], BF16)
        sb("WBUF", [128, 12288], BF16)
        sb("SCR", [128, S + 16], F32)
        sb("SCR2", [128, S], F32)
        sb("SMALL", [128, 512], F32)
        sb("WBUF2", [128, 1024], BF16)
        sb("CONST", [128, 288], BF16)
        sb("CONST2", [128, 272], F32)
        sb("GWT", [128, 64], F32)
        sb("VECS", [128, 128], F32)
        psum("PA", [128, 2048], F32)
        psum("PB", [128, 1024], F32)
        psum("PC", [128, 2048], BF16)

        P = Prog(nc, st)
        global _LAST_P
        _LAST_P = P
        block = st.enter_context(nc.Block())
        emit_all(nc, P, dram, sbufs, out, tap_aps, upto)
        P.finalize(block)
    return nc


_LAST_P = None


def emit_all(nc, P, dram, sbufs, out, tap_aps, upto):
    RES = sbufs["RES"]; FB = sbufs["FB"]; BFX = sbufs["BFX"]
    WBUF = sbufs["WBUF"]; WBUF2 = sbufs["WBUF2"]
    SCR = sbufs["SCR"]; SCR2 = sbufs["SCR2"]
    SMALL = sbufs["SMALL"]; CONST = sbufs["CONST"]
    PA = sbufs["PA"]; PB = sbufs["PB"]; PC = sbufs["PC"]
    xr = dram["xr"]

    def b_memset(g):
        return g.memset(CONST[:, 0:128], 0.0)
    P.op("gpsimd", b_memset, writes=["CONST"])
    def b_diag(g):
        return g.affine_select(out=CONST[:, 0:128], in_=CONST[:, 0:128],
                               compare_op=ALU.not_equal, fill=1.0, base=0,
                               pattern=[[-1, 128]], channel_multiplier=1)
    P.op("gpsimd", b_diag, reads=["CONST"], writes=["CONST"])
    def b_ones(g):
        return g.memset(CONST[:, 128:130], 1.0)
    P.op("gpsimd", b_ones, reads=["CONST"], writes=["CONST"])
    def b_onesr(g):
        return g.memset(CONST[0:1, 160:288], 1.0)
    P.op("gpsimd", b_onesr, reads=["CONST"], writes=["CONST"])
    IDENT = CONST[:, 0:128]
    ONESC = CONST[:, 128:129]
    ONESR = CONST[0:1, 160:288]
    CONST2 = sbufs["CONST2"]; GWT = sbufs["GWT"]; VECS = sbufs["VECS"]
    def b_c2z(g):
        return g.memset(CONST2[:, 0:128], 0.0)
    P.op("gpsimd", b_c2z, writes=["CONST2"])
    def b_c2d(g):
        return g.affine_select(out=CONST2[:, 0:128], in_=CONST2[:, 0:128],
                               compare_op=ALU.not_equal, fill=1.0, base=0,
                               pattern=[[-1, 128]], channel_multiplier=1)
    P.op("gpsimd", b_c2d, reads=["CONST2"], writes=["CONST2"])
    def b_c2o(g):
        return g.memset(CONST2[:, 128:130], 1.0)
    P.op("gpsimd", b_c2o, reads=["CONST2"], writes=["CONST2"])
    def b_c2r(g):
        return g.memset(CONST2[0:1, 144:272], 1.0)
    P.op("gpsimd", b_c2r, reads=["CONST2"], writes=["CONST2"])
    IDF = CONST2[:, 0:128]
    ONESC_F = CONST2[:, 128:129]
    ONESR_F = CONST2[0:1, 144:272]

    x_src = xr.rearrange("b (t p) d -> b t p d", p=128)

    P.dma("sync", RES.rearrange("p (t d) -> p t d", d=D),
          xr.rearrange("b (t p) d -> b p t d", p=128)[0], writes=["RES"])
    def b_cast(v):
        return v.tensor_copy(BFX[:, :], RES[:, :])
    P.op("vector", b_cast, reads=["RES"], writes=["BFX"])

    def emit_fold(src_ap, src_keys, mt, first):
        def b1(v, src_ap=src_ap):
            return v.tensor_add(SMALL[:, 0:512], src_ap[:, 0:512], src_ap[:, 512:1024])
        P.op("vector", b1, reads=list(src_keys), writes=["SMALL"])
        def b2(v):
            return v.tensor_add(SMALL[:, 0:256], SMALL[:, 0:256], SMALL[:, 256:512])
        P.op("vector", b2, reads=["SMALL"], writes=["SMALL"])
        def b3(v):
            return v.tensor_add(SMALL[:, 0:128], SMALL[:, 0:128], SMALL[:, 128:256])
        P.op("vector", b3, reads=["SMALL"], writes=["SMALL"])
        mcol = SCR2[:, mt * 64:(mt + 1) * 64]
        if first:
            def b4(v, mcol=mcol):
                return v.tensor_add(mcol, SMALL[:, 0:64], SMALL[:, 64:128])
            P.op("vector", b4, reads=["SMALL"], writes=["m_acc"], inc=True)
        else:
            def b4(v):
                return v.tensor_add(SMALL[:, 0:64], SMALL[:, 0:64], SMALL[:, 64:128])
            P.op("vector", b4, reads=["SMALL"], writes=["SMALL"])
            def b5(v, mcol=mcol):
                return v.tensor_add(mcol, mcol, SMALL[:, 0:64])
            P.op("vector", b5, reads=["SMALL", "m_acc"], writes=["m_acc"], inc=True)

    for mt in range(NT):
        emit_fold(RES[:, mt * D:(mt + 1) * D], ["RES"], mt, first=True)
    for b in range(1, B):
        for mt in range(NT):
            slot = (b * NT + mt) % 2
            q = "sync" if slot == 0 else "scalar"
            dst = SCR[:, slot * 1024:(slot + 1) * 1024]
            P.dma(q, dst, x_src[b, mt], writes=[f"xs{slot}"])
            emit_fold(dst, [f"xs{slot}"], mt, first=False)

    def b_scale(v):
        return v.tensor_scalar_mul(SCR2[:, 0:1024], SCR2[:, 0:1024], 1.0 / 128.0)
    P.op("vector", b_scale, reads=["m_acc"], writes=["m_acc"])
    def b_mcast(v):
        return v.tensor_copy(WBUF[:, 0:1024], SCR2[:, 0:1024])
    P.op("vector", b_mcast, reads=["m_acc"], writes=["WBUFh"])
    for mt in range(NT):
        def b_tr(t, mt=mt):
            return t.transpose(PC[0:64, mt * 128:(mt + 1) * 128],
                               WBUF[:, mt * 64:(mt + 1) * 64], IDENT)
        P.op("tensor", b_tr, reads=["WBUFh", "CONST"], writes=["PC"],
             inc=(mt == NT - 1), pe_chain=True)
    def b_mTc(v):
        return v.tensor_copy(WBUF[0:64, 1024:3072], PC[0:64, 0:2048])
    P.op("vector", b_mTc, reads=["PC"], writes=["WBUFh"])
    MT_ = WBUF[0:64, 1024:3072]

    for kt in range(NT):
        for half in range(2):
            def b_s(t, kt=kt, half=half):
                return t.matmul(PB[0:1, half * 512:(half + 1) * 512], ONESC,
                                BFX[:, kt * D + half * 512:kt * D + (half + 1) * 512],
                                start=(kt == 0), stop=(kt == NT - 1))
            P.op("tensor", b_s, reads=["BFX", "CONST"], writes=["PB"],
                 inc=(kt == NT - 1 and half == 1), pe_chain=True)
    def b_sev(v):
        return v.tensor_copy(SCR[0:1, 0:1024], PB[0:1, 0:1024])
    P.op("vector", b_sev, reads=["PB"], writes=["SCR", "Srow", "xs0", "xs1"])
    def b_sbf(v):
        return v.tensor_copy(WBUF[0:1, 3072:4096], SCR[0:1, 0:1024])
    P.op("vector", b_sbf, reads=["SCR", "Srow"], writes=["WBUFh"])
    SROW_BF = WBUF[0:1, 3072:4096]

    for it in range(NT):
        for ncol in range(4):
            def b_mm(t, it=it, ncol=ncol):
                return t.matmul(PA[:, ncol * 512:(ncol + 1) * 512],
                                MT_[:, it * 128:(it + 1) * 128],
                                MT_[:, ncol * 512:(ncol + 1) * 512],
                                start=True, stop=True)
            P.op("tensor", b_mm, reads=["WBUFh"], writes=["PA"],
                 inc=(ncol == 3), pe_chain=True)
        def b_clip(v):
            return v.tensor_scalar(SCR[:, 0:2048], PA[:, :], PI, -PI, ALU.min, ALU.max)
        P.op("vector", b_clip, reads=["PA"], writes=["SCR"])
        def b_sin(s):
            return s.activation(SCR[:, 0:2048], SCR[:, 0:2048], AF.Sin, scale=0.5)
        P.op("scalar", b_sin, reads=["SCR"], writes=["SCR"])
        def b_sq(s, it=it):
            return s.activation(FB[:, it * 2048:(it + 1) * 2048], SCR[:, 0:2048],
                                AF.Square)
        P.op("scalar", b_sq, reads=["SCR"], writes=["FB"])
    for it in range(NT):
        def b_exp(s, it=it):
            return s.activation(SCR[:, 0:2048], FB[:, it * 2048:(it + 1) * 2048],
                                AF.Exp,
                                accum_out=SCR2[:, 1024 + it:1024 + it + 1])
        P.op("scalar", b_exp, reads=["FB"], writes=["SCR", "Dcol"])
        def b_expt(s_, it=it):
            return s_.copy(SCR2[:, 1024 + it:1024 + it + 1],
                           SCR2[:, 1024 + it:1024 + it + 1])
        P.op("scalar", b_expt, reads=["Dcol"], writes=["Dcol"])
        def b_f(v, it=it):
            return v.tensor_scalar_add(FB[:, it * 2048:(it + 1) * 2048],
                                       SCR[:, 0:2048], -1.0)
        P.op("vector", b_f, reads=["SCR"], writes=["FB"])
    def b_rcp(v):
        return v.reciprocal(SCR2[:, 1040:1056], SCR2[:, 1024:1040])
    P.op("vector", b_rcp, reads=["Dcol"], writes=["Drec"])
    if "dbg_scr2" in tap_aps:
        P.dma("gpsimd", tap_aps["dbg_scr2"], SCR2[:, 0:2048],
              reads=["m_acc", "Dcol", "Drec"])
    if "dbg_F0" in tap_aps:
        P.dma("gpsimd", tap_aps["dbg_F0"], FB[:, 0:2048], reads=["FB"])
    if "dbg_F5" in tap_aps:
        P.dma("gpsimd", tap_aps["dbg_F5"], FB[:, 5*2048:6*2048], reads=["FB"])

    for it in range(NT):
        for half in range(2):
            def b_s0(t, half=half):
                return t.matmul(PB[:, half * 512:(half + 1) * 512], ONESR,
                                SROW_BF[:, half * 512:(half + 1) * 512],
                                start=True, stop=False)
            P.op("tensor", b_s0, reads=["WBUFh", "CONST"], writes=["PB"],
                 pe_chain=True, inc=False)
        for jt in range(NT):
            for half in range(2):
                def b_o(t, it=it, jt=jt, half=half):
                    return t.matmul(
                        PB[:, half * 512:(half + 1) * 512],
                        FB[:, jt * 2048 + it * 128:jt * 2048 + (it + 1) * 128],
                        BFX[:, jt * D + half * 512:jt * D + (half + 1) * 512],
                        start=False, stop=(jt == NT - 1))
                P.op("tensor", b_o, reads=["FB", "BFX"], writes=["PB"],
                     pe_chain=True, inc=(jt == NT - 1 and half == 1))
        def b_y1(v, it=it):
            return v.scalar_tensor_tensor(
                RES[:, it * D:(it + 1) * D], PB[:, 0:1024],
                SCR2[:, 1040 + it:1041 + it],
                RES[:, it * D:(it + 1) * D], ALU.mult, ALU.add)
        P.op("vector", b_y1, reads=["PB", "Drec", "RES"], writes=["RES"])

    if "dbg_y1a" in tap_aps:
        P.dma("gpsimd", tap_aps["dbg_y1a"], RES[:, 0:2048], reads=["RES"])
    if "dbg_y1b" in tap_aps:
        P.dma("gpsimd", tap_aps["dbg_y1b"], RES[:, 4*1024:6*1024], reads=["RES"])
    SU0, SQ0, MU0, RS0 = 1056, 1072, 1088, 1104
    if not _DBG.get("skip_ln"):
        _ln_token_major(P, RES, SCR, SCR2, SU0, SQ0, MU0, RS0, EPS_LN)

    if "x1_tok" in tap_aps:
        tp = tap_aps["x1_tok"].rearrange("(t p) d -> t p d", p=128)
        for it in range(NT):
            P.dma("sync", tp[it], RES[:, it * D:(it + 1) * D], reads=["RES"])

    if "dbg_ln" in tap_aps:
        P.dma("gpsimd", tap_aps["dbg_ln"], SCR2[:, 0:2048],
              reads=["lnsu", "lnsq", "lnmu", "lnrs"])
    if upto == 1:
        op_t = out.rearrange("(t p) d -> t p d", p=128)
        for it in range(NT):
            P.dma("sync", op_t[it], RES[:, it * D:(it + 1) * D], reads=["RES"])
        return

    P.barrier()
    P.dma("sync", GWT[:, 0:64].rearrange("p (t e) -> p t e", e=8),
          dram["l0_gate_wT"].rearrange("(t p) e -> p t e", p=128),
          writes=["GWT"])
    P.dma("sync", VECS[0:8, 0:1], dram["l0_gate_b"], writes=["VECS0"])

    _flip_res(nc, P, RES, PA, IDF, tok_to_feat=True)
    def b_cast2(v):
        return v.tensor_copy(BFX[:, :], RES[:, :])
    P.op("vector", b_cast2, reads=["RES"], writes=["BFX"])

    for dt in range(ND):
        for c4 in range(4):
            def b_lg(t, dt=dt, c4=c4):
                return t.matmul(PA[0:8, c4 * 512:(c4 + 1) * 512],
                                GWT[:, dt * 8:(dt + 1) * 8],
                                RES[:, dt * S + c4 * 512:dt * S + (c4 + 1) * 512],
                                start=(dt == 0), stop=(dt == ND - 1))
            P.op("tensor", b_lg, reads=["GWT", "RES"], writes=["PA"],
                 pe_chain=True, inc=(dt == ND - 1 and c4 == 3))
    def b_lgev(v):
        return v.tensor_scalar_add(SCR[0:8, 0:2048], PA[0:8, 0:2048],
                                   VECS[0:8, 0:1])
    P.op("vector", b_lgev, reads=["PA", "VECS0"], writes=["SCR", "lgT"])
    for it in range(NT):
        def b_lt(t, it=it):
            return t.transpose(PB[:, it * 8:(it + 1) * 8],
                               SCR[0:8, it * 128:(it + 1) * 128], IDF[0:8, 0:8])
        P.op("tensor", b_lt, reads=["SCR", "CONST2"], writes=["PB"],
             pe_chain=True, inc=(it == NT - 1))
    def b_ltev(v):
        return v.tensor_copy(SCR2[:, 0:128], PB[:, 0:128])
    P.op("vector", b_ltev, reads=["PB"], writes=["ltok"])
    _top2(nc, P, sbufs, "ltok")
    for it in range(NT):
        def b_cT(t, it=it):
            return t.transpose(PA[0:8, it * 128:(it + 1) * 128],
                               SCR2[:, 608 + it * 8:616 + it * 8], IDF)
        P.op("tensor", b_cT, reads=["comb", "CONST2"], writes=["PA"],
             pe_chain=True, inc=(it == NT - 1))
    def b_cTev(v):
        return v.tensor_copy(SCR2[0:8, 0:2048], PA[0:8, 0:2048])
    P.op("vector", b_cTev, reads=["PA"],
         writes=["combT", "ltok", "comb", "t2scr"])

    if "dbg_comb" in tap_aps:
        P.dma("gpsimd", tap_aps["dbg_comb"], SCR2[:, 0:2048],
              reads=["comb", "combT", "t2m1", "t2m2", "ltok"])
    if "dbg_flip" in tap_aps:
        P.dma("gpsimd", tap_aps["dbg_flip"], RES[:, 0:2048], reads=["RES"])
    WD0 = 0
    HH0 = 22528
    wgT_src = dram["l0_wgT"].rearrange("e (t p) h -> e p t h", p=128)
    wuT_src = dram["l0_wuT"].rearrange("e (t p) h -> e p t h", p=128)
    wdT_src = dram["l0_wdT"].rearrange("e (c p) d -> e p c d", p=128)

    def hh_ap(ht):
        if ht < 20:
            return FB[:, HH0 + ht * 512:HH0 + (ht + 1) * 512]
        return WBUF2[:, (ht - 20) * 512:(ht - 19) * 512]

    NCH = 11
    P.join("wd", ["FB"])
    for ht in range(NH0):
        P.join(f"hh{ht}", ["FB"])
    for sk in ("w0", "w1", "w2", "w3", "w4", "w5"):
        P.join(sk, ["WBUFh"])
    n_exp = _DBG.get("n_experts", E0) if not _DBG.get("skip_experts") else 0
    for e in range(n_exp):
        P.dma("gpsimd", FB[:, 0:22528].rearrange("p (c d) -> p c d", d=1024),
              wdT_src[e], reads=(), writes=["wd"])
        P.dma("sync", SCR[0:1, 0:2048], SCR2[e:e + 1, 0:2048],
              reads=["combT"], writes=["combrow", "SCR", "combbc"])
        for half in range(2):
            for c2 in range(2):
                def b_cb(t, half=half, c2=c2):
                    return t.matmul(PB[:, c2 * 512:(c2 + 1) * 512], ONESR_F,
                                    SCR[0:1, half * 1024 + c2 * 512:
                                        half * 1024 + (c2 + 1) * 512],
                                    start=True, stop=True)
                P.op("tensor", b_cb, reads=["combrow", "CONST2"], writes=["PB"],
                     pe_chain=True, inc=(c2 == 1))
            def b_cbe(v, half=half):
                return v.tensor_copy(SCR[:, half * 1024:(half + 1) * 1024],
                                     PB[:, 0:1024])
            P.op("vector", b_cbe, reads=["PB"], writes=["SCR", "combbc"])

        def pair_dma(pos):
            ci = pos % NCH
            sg = (2 * pos) % 6
            su = (2 * pos + 1) % 6
            P.dma("gpsimd",
                  WBUF[:, sg * 2048:(sg + 1) * 2048].rearrange(
                      "p (t h) -> p t h", h=256),
                  wgT_src[e, :, :, ci * 256:(ci + 1) * 256],
                  writes=[f"w{sg}"])
            P.dma("gpsimd",
                  WBUF[:, su * 2048:(su + 1) * 2048].rearrange(
                      "p (t h) -> p t h", h=256),
                  wuT_src[e, :, :, ci * 256:(ci + 1) * 256],
                  writes=[f"w{su}"])
        pair_dma(0)
        pair_dma(1)
        for tb in range(4):
            t0 = tb * 512
            for ht in range(NH0):
                ci = ht // 2
                pos = tb * NCH + ci
                if ht % 2 == 0 and pos + 2 < 4 * NCH:
                    pair_dma(pos + 2)
                sg = (2 * pos) % 6
                su = (2 * pos + 1) % 6
                hofs = (ht % 2) * 128
                if ht % 2 == 0:
                    gps, gk = PA[:, 0:512], "PAg"
                    ups, uk = PA[:, 512:1024], "PAu"
                else:
                    gps, gk = PB[:, 0:512], "PBg"
                    ups, uk = PB[:, 512:1024], "PBu"
                for dt in range(ND):
                    def b_g(t, dt=dt, sg=sg, hofs=hofs, gps=gps, t0=t0):
                        return t.matmul(
                            gps,
                            WBUF[:, sg * 2048 + dt * 256 + hofs:
                                 sg * 2048 + dt * 256 + hofs + 128],
                            BFX[:, dt * S + t0:dt * S + t0 + 512],
                            start=(dt == 0), stop=(dt == ND - 1))
                    P.op("tensor", b_g, reads=[f"w{sg}", "BFX"], writes=[gk],
                         pe_chain=True, inc=(dt == ND - 1))
                for dt in range(ND):
                    def b_u(t, dt=dt, su=su, hofs=hofs, ups=ups, t0=t0):
                        return t.matmul(
                            ups,
                            WBUF[:, su * 2048 + dt * 256 + hofs:
                                 su * 2048 + dt * 256 + hofs + 128],
                            BFX[:, dt * S + t0:dt * S + t0 + 512],
                            start=(dt == 0), stop=(dt == ND - 1))
                    P.op("tensor", b_u, reads=[f"w{su}", "BFX"], writes=[uk],
                         pe_chain=True, inc=(dt == ND - 1))
                def b_silu(s_, gps=gps):
                    return s_.activation(SMALL[:, 0:512], gps, AF.Silu)
                P.op("scalar", b_silu, reads=[gk], writes=["SMALL"])
                def b_hh(v, ups=ups, ht=ht):
                    return v.tensor_mul(hh_ap(ht), SMALL[:, 0:512], ups)
                P.op("vector", b_hh, reads=["SMALL", uk], writes=[f"hh{ht}"])
            if e == 0 and tb == 0 and "dbg_hh" in tap_aps:
                P.dma("gpsimd", tap_aps["dbg_hh"], FB[:, HH0:HH0 + 2048],
                      reads=["hh0", "hh1", "hh2", "hh3"])
            if e == 0 and tb == 0 and "dbg_wg" in tap_aps:
                P.dma("gpsimd", tap_aps["dbg_wg"], WBUF[:, 0:2048],
                      reads=["wg0"])
            if e == 0 and tb == 0 and "dbg_wd" in tap_aps:
                P.dma("gpsimd", tap_aps["dbg_wd"], FB[:, 0:2048],
                      reads=["wd"])
            for dt in range(ND):
                eo, ek = (PA[:, 1024:1536], "PAe0") if dt % 2 == 0 else                          (PA[:, 1536:2048], "PAe1")
                for ht in range(NH0):
                    def b_eo(t, dt=dt, ht=ht, eo=eo):
                        return t.matmul(
                            eo, FB[:, ht * 1024 + dt * 128:ht * 1024 + dt * 128 + 128],
                            hh_ap(ht),
                            start=(ht == 0), stop=(ht == NH0 - 1))
                    P.op("tensor", b_eo, reads=["wd", f"hh{ht}"], writes=[ek],
                         pe_chain=True, inc=(ht == NH0 - 1))
                def b_cm(v, eo=eo, t0=t0):
                    return v.tensor_mul(SMALL[:, 0:512], eo,
                                        SCR[:, t0:t0 + 512])
                P.op("vector", b_cm, reads=[ek, "combbc"], writes=["SMALL"])
                def b_acc(v, dt=dt, t0=t0):
                    return v.tensor_add(RES[:, dt * S + t0:dt * S + t0 + 512],
                                        RES[:, dt * S + t0:dt * S + t0 + 512],
                                        SMALL[:, 0:512])
                P.op("vector", b_acc, reads=["RES", "SMALL"], writes=["RES"])

    if "dbg_y2" in tap_aps:
        P.dma("gpsimd", tap_aps["dbg_y2"], RES[:, 0:2048], reads=["RES"])
    if "dbg_y2b" in tap_aps:
        P.dma("gpsimd", tap_aps["dbg_y2b"], RES[:, 5 * 2048:6 * 2048],
              reads=["RES"])
    _ln_feature_major(nc, P, sbufs, EPS_LN, with_mean=True)

    if upto == 2:
        flat = out.rearrange("(dt p two) d -> dt p (two d)", p=128, two=2)
        for dt in range(ND):
            P.dma("sync", flat[dt], RES[:, dt * S:(dt + 1) * S], reads=["RES"])
        return

    P.barrier()
    binr = dram["l1_bin"].rearrange("(a one) o -> a (one o)", one=1)
    for ct in range(17):
        n = 128 if ct < 16 else 16
        P.dma("scalar", VECS[0:n, 1 + ct:2 + ct],
              dram["l1_bin"][ct * 128:ct * 128 + n, 0:1], writes=["VECSbin"])
    for ct in range(16):
        P.dma("scalar", VECS[:, 18 + ct:19 + ct],
              dram["l1_bconv"][ct * 128:(ct + 1) * 128, 0:1],
              writes=["VECSbconv"])
        P.dma("scalar", VECS[:, 34 + ct * 4:38 + ct * 4],
              dram["l1_wconv"][ct * 128:(ct + 1) * 128, :],
              writes=["VECSwconv"])
    P.dma("scalar", VECS[0:16, 98:99], dram["l1_bdt"], writes=["VECSbdt"])

    def b_cast3(v):
        return v.tensor_copy(BFX[:, :], RES[:, :])
    P.op("vector", b_cast3, reads=["RES"], writes=["BFX"])

    for k, old_ in (("win0", "w0"), ("win1", "w0"), ("x96w", "w1"),
                    ("wo16", "w1"), ("wo16", "w2"), ("y16bf", "w2"),
                    ("y16bf", "w3")):
        P.join(k, [k, old_] if k in P.last_write else [old_])
    for ct in range(16):
        P.join(f"xcbf{ct}", ["wd"] + [f"hh{h}" for h in range(20)])

    P.dma("gpsimd", WBUF[:, 2048:3584].rearrange("p (c m) -> p c m", m=96),
          dram["l1_wx96T"].rearrange("(c p) m -> p c m", p=128),
          reads=["x96w"], writes=["x96w"])
    P.dma("gpsimd", WBUF[0:16, 3584:4608], dram["l1_wout16T"],
          reads=["wo16"], writes=["wo16"])

    win_src = dram["l1_winT"].rearrange("(t p) c -> p t c", p=128)
    def win_dma(ct):
        slot = ct % 2
        n = 128 if ct < 16 else 16
        P.dma("gpsimd",
              WBUF[:, slot * 1024:slot * 1024 + 8 * n].rearrange(
                  "p (t c) -> p t c", c=n),
              win_src[:, :, ct * 128:ct * 128 + n],
              reads=[f"win{slot}"], writes=[f"win{slot}"])
    def b_pad(g):
        return g.memset(SCR[:, 0:3], 0.0)
    P.op("gpsimd", b_pad, reads=["SCR"], writes=["SCR", "scrpad"])

    win_dma(0)
    for ct in range(16):
        if ct + 1 < 16:
            win_dma(ct + 1)
        slot = ct % 2
        for c4 in range(4):
            for dt in range(ND):
                def b_xz(t, c4=c4, dt=dt, slot=slot):
                    return t.matmul(
                        PA[:, c4 * 512:(c4 + 1) * 512],
                        WBUF[:, slot * 1024 + dt * 128:slot * 1024 + (dt + 1) * 128],
                        BFX[:, dt * S + c4 * 512:dt * S + (c4 + 1) * 512],
                        start=(dt == 0), stop=(dt == ND - 1))
                P.op("tensor", b_xz, reads=[f"win{slot}", "BFX"], writes=["PA"],
                     pe_chain=True, inc=(dt == ND - 1))
        def b_xe(v, ct=ct):
            return v.tensor_scalar_add(SCR[:, 3:2051], PA[:, 0:2048],
                                       VECS[:, 1 + ct:2 + ct])
        P.op("vector", b_xe, reads=["PA", "VECSbin", "scrpad"],
             writes=["SCR", "xcpre"])
        def b_p1(s_, ct=ct):
            return s_.activation(SCR2[:, 0:2048], SCR[:, 1:2049], AF.Identity
                                 if False else AF.Copy,
                                 scale=VECS[:, 35 + ct * 4:36 + ct * 4])
        P.op("scalar", b_p1, reads=["SCR", "VECSwconv"],
             writes=["SCR2", "convacc"])
        for eng, k in (("vector", 0), ("vector", 2), ("vector", 3)):
            def b_tap(v, k=k, ct=ct):
                return v.scalar_tensor_tensor(
                    SCR2[:, 0:2048], SCR[:, k:k + 2048],
                    VECS[:, 34 + ct * 4 + k:35 + ct * 4 + k],
                    SCR2[:, 0:2048], ALU.mult, ALU.add)
            P.op(eng, b_tap, reads=["SCR", "VECSwconv", "convacc"],
                 writes=["SCR2", "convacc"])
        def b_silu3(s_, ct=ct):
            return s_.activation(FB[:, ct * 2048:(ct + 1) * 2048],
                                 SCR2[:, 0:2048], AF.Silu,
                                 bias=VECS[:, 18 + ct:19 + ct])
        P.op("scalar", b_silu3, reads=["SCR2", "convacc", "VECSbconv"],
             writes=[f"xcbf{ct}"])

    for half in range(2):
        for ct in range(16):
            for c2 in range(2):
                def b_x96(t, half=half, ct=ct, c2=c2):
                    return t.matmul(
                        PB[0:96, c2 * 512:(c2 + 1) * 512],
                        WBUF[:, 2048 + ct * 96:2048 + (ct + 1) * 96],
                        FB[:, ct * 2048 + half * 1024 + c2 * 512:
                           ct * 2048 + half * 1024 + (c2 + 1) * 512],
                        start=(ct == 0), stop=(ct == 15))
                P.op("tensor", b_x96, reads=["x96w", f"xcbf{ct}"], writes=["PB"],
                     pe_chain=True, inc=(ct == 15 and c2 == 1))
        def b_x96e(v, half=half):
            return v.tensor_copy(SCR2[0:96, half * 1024:(half + 1) * 1024],
                                 PB[0:96, 0:1024])
        P.op("vector", b_x96e, reads=["PB"],
             writes=["x96", "SCR2", "convacc"])
    def b_bdt(v):
        return v.tensor_scalar_add(SCR2[64:80, 0:2048], SCR2[64:80, 0:2048],
                                   VECS[0:16, 98:99])
    P.op("vector", b_bdt, reads=["x96", "VECSbdt"], writes=["x96"])

    if "dbg_xc0" in tap_aps:
        P.dma("gpsimd", tap_aps["dbg_xc0"], FB[:, 0:2048], reads=["xcbf0"])
    if "dbg_xc9" in tap_aps:
        P.dma("gpsimd", tap_aps["dbg_xc9"], FB[:, 9 * 2048:10 * 2048],
              reads=["xcbf9"])
    if "dbg_x96" in tap_aps:
        P.dma("gpsimd", tap_aps["dbg_x96"], SCR2[:, 0:2048], reads=["x96"])
    def b_abs(s_):
        return s_.activation(SCR[64:80, 0:2048], SCR2[64:80, 0:2048], AF.Abs)
    P.op("scalar", b_abs, reads=["x96"], writes=["SCR", "sp1"])
    def b_expn(s_):
        return s_.activation(SCR[64:80, 0:2048], SCR[64:80, 0:2048],
                             AF.Exp, scale=-1.0)
    P.op("scalar", b_expn, reads=["sp1"], writes=["sp1"])
    def b_ln1p(s_):
        return s_.activation(SCR[64:80, 0:2048], SCR[64:80, 0:2048],
                             AF.Ln, bias=1.0)
    P.op("scalar", b_ln1p, reads=["sp1"], writes=["sp1"])
    def b_relu(s_):
        return s_.activation(PA[0:16, 0:2048], SCR2[64:80, 0:2048], AF.Relu)
    P.op("scalar", b_relu, reads=["x96", "PA"], writes=["PA", "sp2"])
    def b_delta(v):
        return v.tensor_add(PA[64:80, 0:2048], SCR[64:80, 0:2048],
                            PA[0:16, 0:2048])
    P.op("vector", b_delta, reads=["sp1", "sp2", "PA"], writes=["PA", "delta"])

    def b_absA(s_):
        return s_.activation(SCR[32:48, 0:2048], SCR2[0:16, 0:2048], AF.Abs)
    P.op("scalar", b_absA, reads=["x96"], writes=["SCR", "Apre"])
    def b_Amul(v):
        return v.tensor_mul(SCR[32:48, 0:2048], SCR[32:48, 0:2048],
                            PA[64:80, 0:2048])
    P.op("vector", b_Amul, reads=["Apre", "delta"], writes=["Apre"])
    def b_Aexp(s_):
        return s_.activation(SCR[32:48, 0:2048], SCR[32:48, 0:2048],
                             AF.Exp, scale=-1.0)
    P.op("scalar", b_Aexp, reads=["Apre"], writes=["Apre"])

    def b_xc16(v):
        return v.tensor_copy(PA[96:112, 0:2048], FB[0:16, 0:2048])
    P.op("vector", b_xc16, reads=["xcbf0", "PA"], writes=["PA", "xc16"])
    def b_bx1(v):
        return v.tensor_mul(SCR2[32:48, 0:2048], SCR2[32:48, 0:2048],
                            PA[64:80, 0:2048])
    P.op("vector", b_bx1, reads=["x96", "delta"], writes=["x96", "Bx"])
    def b_bx2(v):
        return v.tensor_mul(SCR2[32:48, 0:2048], SCR2[32:48, 0:2048],
                            PA[96:112, 0:2048])
    P.op("vector", b_bx2, reads=["Bx", "xc16"], writes=["Bx"])

    def b_scan(v):
        return v.tensor_tensor_scan(PA[0:16, 0:2048], SCR[32:48, 0:2048],
                                    SCR2[32:48, 0:2048], 0.0,
                                    ALU.mult, ALU.add)
    P.op("vector", b_scan, reads=["Apre", "Bx", "sp2", "PA"],
         writes=["PA", "hscan"])

    win_dma(16)
    for half in range(2):
        for c2 in range(2):
            for dt in range(ND):
                def b_z(t, half=half, c2=c2, dt=dt):
                    return t.matmul(
                        PB[0:16, c2 * 512:(c2 + 1) * 512],
                        WBUF[:, dt * 16:(dt + 1) * 16],
                        BFX[:, dt * S + half * 1024 + c2 * 512:
                            dt * S + half * 1024 + (c2 + 1) * 512],
                        start=(dt == 0), stop=(dt == ND - 1))
                P.op("tensor", b_z, reads=["win0", "BFX"], writes=["PB"],
                     pe_chain=True, inc=(dt == ND - 1))
        def b_ze(v, half=half):
            return v.tensor_scalar_add(
                SCR2[64:80, half * 1024:(half + 1) * 1024],
                PB[0:16, 0:1024], VECS[0:16, 17:18])
        P.op("vector", b_ze, reads=["PB", "VECSbin", "x96"],
             writes=["z16", "x96"])
    def b_y16(v):
        return v.tensor_mul(SCR[0:16, 0:2048], PA[0:16, 0:2048],
                            SCR2[64:80, 0:2048])
    P.op("vector", b_y16, reads=["hscan", "z16", "delta", "SCR"],
         writes=["SCR", "y16"])
    def b_y16b(v):
        return v.tensor_copy(WBUF[0:16, 4608:6656], SCR[0:16, 0:2048])
    P.op("vector", b_y16b, reads=["y16"], writes=["y16bf"])

    if "dbg_y16" in tap_aps:
        P.dma("gpsimd", tap_aps["dbg_y16"], SCR[:, 0:2048],
              reads=["y16", "Apre", "sp1"])
    if "dbg_z16" in tap_aps:
        P.dma("gpsimd", tap_aps["dbg_z16"], SCR2[:, 0:2048],
              reads=["z16", "x96", "Bx"])
    for dt in range(ND):
        for c4 in range(4):
            def b_wo(t, dt=dt, c4=c4):
                return t.matmul(PA[:, c4 * 512:(c4 + 1) * 512],
                                WBUF[0:16, 3584 + dt * 128:3584 + (dt + 1) * 128],
                                WBUF[0:16, 4608 + c4 * 512:4608 + (c4 + 1) * 512],
                                start=True, stop=True)
            P.op("tensor", b_wo, reads=["wo16", "y16bf"], writes=["PA"],
                 pe_chain=True, inc=(c4 == 3))
        def b_res3(v, dt=dt):
            return v.tensor_add(RES[:, dt * S:(dt + 1) * S],
                                RES[:, dt * S:(dt + 1) * S], PA[:, 0:2048])
        P.op("vector", b_res3, reads=["RES", "PA"], writes=["RES"])

    _ln_feature_major(nc, P, sbufs, EPS_LN, with_mean=True)

    if upto == 3:
        flat = out.rearrange("(dt p two) d -> dt p (two d)", p=128, two=2)
        for dt in range(ND):
            P.dma("sync", flat[dt], RES[:, dt * S:(dt + 1) * S], reads=["RES"])
        return

    P.barrier()
    for dt in range(ND):
        P.dma("scalar", VECS[:, 99 + dt:100 + dt],
              dram["l1f_g1"][dt * 128:(dt + 1) * 128, 0:1], writes=["VECSg1"])
        P.dma("scalar", VECS[:, 107 + dt:108 + dt],
              dram["l1f_g2"][dt * 128:(dt + 1) * 128, 0:1], writes=["VECSg2"])
    P.dma("scalar", VECS[0:8, 115:116], dram["l1f_ebias"], writes=["VECSeb"])

    P.dma("sync", dram["x3scr"], RES[:, :], reads=["RES"], writes=["x3scr"])

    _ln_feature_major(nc, P, sbufs, EPS_RMS, with_mean=False, normalize=False)
    for dt in range(ND):
        def b_nx(v, dt=dt):
            return v.scalar_tensor_tensor(BFX[:, dt * S:(dt + 1) * S],
                                          RES[:, dt * S:(dt + 1) * S],
                                          VECS[:, 99 + dt:100 + dt],
                                          SCR[:, 0:2048], ALU.mult, ALU.mult)
        P.op("vector", b_nx, reads=["RES", "VECSg1", "rstdbc"], writes=["BFX"])

    def wslot(pos):
        return pos % 2
    def wchunk_dma(pos, src_ap, width, key, nslots=2):
        slot = pos % nslots
        P.dma("gpsimd",
              WBUF[:, slot * 2048:slot * 2048 + 8 * width].rearrange(
                  "p (t c) -> p t c", c=width),
              src_ap, reads=[f"w{slot}"], writes=[f"w{slot}", key])
        return slot

    wq_src = dram["l1f_WqT"].rearrange("(t p) c -> p t c", p=128)
    wkc_src = dram["l1f_WkcT"].rearrange("(t p) c -> p t c", p=128)
    wvc_src = dram["l1f_WvcT"].rearrange("(t p) c -> p t c", p=128)
    wo_src = dram["l1f_WoT"].rearrange("(t p) c -> p t c", p=128)
    pos = [0]
    def stream_proj(src, nt_out, fb0, key):
        for i in range(nt_out):
            slot = wchunk_dma(pos[0], src[:, :, i * 128:(i + 1) * 128], 128, key)
            pos[0] += 1
            for c4 in range(4):
                for dt in range(ND):
                    def b_p(t, slot=slot, c4=c4, dt=dt):
                        return t.matmul(
                            PA[:, c4 * 512:(c4 + 1) * 512],
                            WBUF[:, slot * 2048 + dt * 128:slot * 2048 + (dt + 1) * 128],
                            BFX[:, dt * S + c4 * 512:dt * S + (c4 + 1) * 512],
                            start=(dt == 0), stop=(dt == ND - 1))
                    P.op("tensor", b_p, reads=[f"w{slot}", key, "BFX"],
                         writes=["PA"], pe_chain=True, inc=(dt == ND - 1))
            def b_pe(v, i=i, fb0=fb0):
                return v.tensor_copy(FB[:, fb0 + i * 2048:fb0 + (i + 1) * 2048],
                                     PA[:, 0:2048])
            P.op("vector", b_pe, reads=["PA"], writes=[f"fb{fb0}_{i}"])
    for i in range(8):
        P.join(f"fb0_{i}", ["wd", "xcbf%d" % i] + [f"xcbf{j}" for j in range(16)])
    for i in range(2):
        P.join(f"fb16384_{i}", [f"xcbf{8 + i}"])
        P.join(f"fb20480_{i}", [f"xcbf{10 + i}"])
    for i in range(2):
        P.join(f"fb24576_{i}", [f"xcbf{12 + i}", f"xcbf{13 + i}", f"xcbf{14 + i}"])
    stream_proj(wq_src, 8, 0, "wq")
    stream_proj(wkc_src, 2, 16384, "wkc")
    stream_proj(wvc_src, 2, 20480, "wvc")

    nvc = 0
    for tt in range(16):
        for vt in range(2):
            pcc = 0 if nvc % 2 == 0 else 1024
            def b_vt(t, vt=vt, tt=tt, pcc=pcc):
                return t.transpose(PC[:, pcc:pcc + 128],
                                   FB[:, 20480 + vt * 2048 + tt * 128:
                                      20480 + vt * 2048 + (tt + 1) * 128],
                                   IDENT)
            P.op("tensor", b_vt, reads=[f"fb20480_{vt}", "CONST"],
                 writes=[f"pcv{pcc}"], pe_chain=False)
            def b_vc(v, vt=vt, tt=tt, pcc=pcc):
                return v.tensor_copy(
                    FB[:, 24576 + tt * 256 + vt * 128:
                       24576 + tt * 256 + (vt + 1) * 128],
                    PC[:, pcc:pcc + 128])
            P.op("vector", b_vc, reads=[f"pcv{pcc}"], writes=["vctok"])
            nvc += 1

    NEG0, NEG1, RS_, RCP_ = 0, 1, 2, 3
    steps = [(h, st) for h in range(4) for st in range(NT)]
    def emit_scores(k):
        h, st = steps[k]
        for tt2 in range(2):
            for c4 in range(4):
                def b_sc(t, h=h, st=st, tt2=tt2, c4=c4):
                    return t.matmul(
                        PA[:, c4 * 512:(c4 + 1) * 512],
                        FB[:, (2 * h + tt2) * 2048 + st * 128:
                           (2 * h + tt2) * 2048 + (st + 1) * 128],
                        FB[:, 16384 + tt2 * 2048 + c4 * 512:
                           16384 + tt2 * 2048 + (c4 + 1) * 512],
                        start=(tt2 == 0), stop=(tt2 == 1))
                P.op("tensor", b_sc,
                     reads=[f"fb0_{2 * h + tt2}", "fb16384_0", "fb16384_1"],
                     writes=["PA"], pe_chain=True,
                     inc=(tt2 == 1 and c4 == 3))
    emit_scores(0)
    for k in range(len(steps)):
        h, st = steps[k]
        def b_mx(v):
            return v.reduce_max(SMALL[:, NEG0:NEG0 + 1], PA[:, 0:2048],
                                axis=AXX)
        P.op("vector", b_mx, reads=["PA"], writes=["SMALL"])
        def b_ng(v):
            return v.tensor_scalar_mul(SMALL[:, NEG1:NEG1 + 1],
                                       SMALL[:, NEG0:NEG0 + 1], -1.0 / 16.0)
        P.op("vector", b_ng, reads=["SMALL"], writes=["SMALL"])
        def b_ex(s_):
            return s_.activation(SCR2[:, 0:2048], PA[:, 0:2048], AF.Exp,
                                 bias=SMALL[:, NEG1:NEG1 + 1],
                                 scale=1.0 / 16.0,
                                 accum_out=SMALL[:, RS_:RS_ + 1])
        P.op("scalar", b_ex, reads=["PA", "SMALL"], writes=["SCR2", "SMALL"])
        def b_ext(s_):
            return s_.copy(SMALL[:, RS_:RS_ + 1], SMALL[:, RS_:RS_ + 1])
        P.op("scalar", b_ext, reads=["SMALL"], writes=["SMALL"])
        if k + 1 < len(steps):
            emit_scores(k + 1)
        def b_rc4(v):
            return v.reciprocal(SMALL[:, RCP_:RCP_ + 1], SMALL[:, RS_:RS_ + 1])
        P.op("vector", b_rc4, reads=["SMALL"], writes=["SMALL"])
        def b_an(v):
            return v.tensor_scalar_mul(WBUF[:, 4096:6144], SCR2[:, 0:2048],
                                       SMALL[:, RCP_:RCP_ + 1])
        P.op("vector", b_an, reads=["SCR2", "SMALL"], writes=["anorm"])
        for tt in range(16):
            def b_tA(t, tt=tt):
                return t.transpose(PC[:, tt * 128:(tt + 1) * 128],
                                   WBUF[:, 4096 + tt * 128:4096 + (tt + 1) * 128],
                                   IDENT)
            P.op("tensor", b_tA, reads=["anorm", "CONST"], writes=["PC"],
                 pe_chain=True, inc=(tt == 15))
        def b_cA(v):
            return v.tensor_copy(WBUF[:, 6144:8192], PC[:, 0:2048])
        P.op("vector", b_cA, reads=["PC"], writes=["atr"])
        for j in range(2):
            for tt in range(16):
                def b_pv(t, j=j, tt=tt):
                    return t.matmul(
                        PB[:, j * 128:(j + 1) * 128],
                        FB[:, 24576 + tt * 256 + j * 128:
                           24576 + tt * 256 + (j + 1) * 128],
                        WBUF[:, 6144 + tt * 128:6144 + (tt + 1) * 128],
                        start=(tt == 0), stop=(tt == 15))
                P.op("tensor", b_pv, reads=["vctok", "atr"], writes=["PB"],
                     pe_chain=True, inc=(tt == 15 and j == 1))
        def b_pve(v, h=h, st=st):
            return v.tensor_copy(
                FB[:, 2 * h * 2048 + st * 128:2 * h * 2048 + (st + 1) * 128],
                PB[:, 0:128])
        P.op("vector", b_pve, reads=["PB"], writes=[f"fb0_{2 * h}"])
        def b_pve2(v, h=h, st=st):
            return v.tensor_copy(
                FB[:, (2 * h + 1) * 2048 + st * 128:
                   (2 * h + 1) * 2048 + (st + 1) * 128],
                PB[:, 128:256])
        P.op("vector", b_pve2, reads=["PB"], writes=[f"fb0_{2 * h + 1}"])

    for dt in range(ND):
        slot = wchunk_dma(pos[0], wo_src[:, :, dt * 128:(dt + 1) * 128], 128,
                          "wo")
        pos[0] += 1
        for c4 in range(4):
            for k in range(ND):
                def b_woM(t, slot=slot, c4=c4, k=k):
                    return t.matmul(
                        PA[:, c4 * 512:(c4 + 1) * 512],
                        WBUF[:, slot * 2048 + k * 128:slot * 2048 + (k + 1) * 128],
                        FB[:, k * 2048 + c4 * 512:k * 2048 + (c4 + 1) * 512],
                        start=(k == 0), stop=(k == ND - 1))
                P.op("tensor", b_woM,
                     reads=[f"w{slot}", "wo"] + [f"fb0_{k}" for k in range(8)],
                     writes=["PA"], pe_chain=True, inc=(k == ND - 1))
        def b_woE(v, dt=dt):
            return v.tensor_add(RES[:, dt * S:(dt + 1) * S],
                                RES[:, dt * S:(dt + 1) * S], PA[:, 0:2048])
        P.op("vector", b_woE, reads=["RES", "PA"], writes=["RES"])

    if "dbg_x4a" in tap_aps:
        P.dma("gpsimd", tap_aps["dbg_x4a"], RES[:, 0:2048], reads=["RES"])
    if "dbg_x4b" in tap_aps:
        P.dma("gpsimd", tap_aps["dbg_x4b"], RES[:, 5 * 2048:6 * 2048],
              reads=["RES"])
    _ln_feature_major(nc, P, sbufs, EPS_RMS, with_mean=False, normalize=False)
    for dt in range(ND):
        def b_xf(v, dt=dt):
            return v.scalar_tensor_tensor(BFX[:, dt * S:(dt + 1) * S],
                                          RES[:, dt * S:(dt + 1) * S],
                                          VECS[:, 107 + dt:108 + dt],
                                          SCR[:, 0:2048], ALU.mult, ALU.mult)
        P.op("vector", b_xf, reads=["RES", "VECSg2", "rstdbc"], writes=["BFX"])

    P.dma("sync", GWT[:, 0:64].rearrange("p (t e) -> p t e", e=8),
          dram["l1f_WrT"].rearrange("(t p) e -> p t e", p=128),
          reads=["GWT"], writes=["GWT"])
    for dt in range(ND):
        def b_wr(v, dt=dt):
            return v.tensor_scalar_mul(GWT[:, dt * 8:(dt + 1) * 8],
                                       GWT[:, dt * 8:(dt + 1) * 8],
                                       VECS[:, 107 + dt:108 + dt])
        P.op("vector", b_wr, reads=["GWT", "VECSg2"], writes=["GWT"])
    for dt in range(ND):
        for c4 in range(4):
            def b_lg2(t, dt=dt, c4=c4):
                return t.matmul(PA[0:8, c4 * 512:(c4 + 1) * 512],
                                GWT[:, dt * 8:(dt + 1) * 8],
                                RES[:, dt * S + c4 * 512:dt * S + (c4 + 1) * 512],
                                start=(dt == 0), stop=(dt == ND - 1))
            P.op("tensor", b_lg2, reads=["GWT", "RES"], writes=["PA"],
                 pe_chain=True, inc=(dt == ND - 1 and c4 == 3))
    def b_lsc(v):
        return v.tensor_mul(SCR[0:8, 0:2048], PA[0:8, 0:2048], SCR[0:8, 0:2048])
    P.op("vector", b_lsc, reads=["PA", "rstdbc", "SCR"], writes=["SCR", "lgT"])
    def b_lbi(v):
        return v.tensor_scalar_add(SCR[0:8, 0:2048], SCR[0:8, 0:2048],
                                   VECS[0:8, 115:116])
    P.op("vector", b_lbi, reads=["lgT", "VECSeb", "SCR"], writes=["SCR", "lgT"])
    for it in range(NT):
        def b_lt2(t, it=it):
            return t.transpose(PB[:, it * 8:(it + 1) * 8],
                               SCR[0:8, it * 128:(it + 1) * 128], IDF[0:8, 0:8])
        P.op("tensor", b_lt2, reads=["lgT", "SCR", "CONST2"], writes=["PB"],
             pe_chain=True, inc=(it == NT - 1))
    def b_lte2(v):
        return v.tensor_copy(SCR2[:, 0:128], PB[:, 0:128])
    P.op("vector", b_lte2, reads=["PB"], writes=["ltok"])
    _top2(nc, P, sbufs, "ltok", mode="soft")
    for it in range(NT):
        def b_cT2(t, it=it):
            return t.transpose(PA[0:8, it * 128:(it + 1) * 128],
                               SCR2[:, 608 + it * 8:616 + it * 8], IDF)
        P.op("tensor", b_cT2, reads=["comb", "CONST2"], writes=["PA"],
             pe_chain=True, inc=(it == NT - 1))
    def b_cTe2(v):
        return v.tensor_copy(SCR2[0:8, 0:2048], PA[0:8, 0:2048])
    P.op("vector", b_cTe2, reads=["PA"],
         writes=["combT", "ltok", "comb", "t2scr"])

    ew1_src = dram["l1f_eW1T"].rearrange("e (t p) f -> e p t f", p=128)
    ew3_src = dram["l1f_eW3T"].rearrange("e (t p) f -> e p t f", p=128)
    ew2_src = dram["l1f_eW2T"].rearrange("e (f p) d -> e p f d", p=128)
    sw1_src = dram["l1f_sW1T"].rearrange("(t p) f -> p t f", p=128)
    sw3_src = dram["l1f_sW3T"].rearrange("(t p) f -> p t f", p=128)
    sw2_src = dram["l1f_sW2T"].rearrange("(f p) d -> p f d", p=128)

    P.join("w2", ["anorm"])
    P.join("w3", ["atr"])
    P.join("vctok", ["vctok"])
    P.join("ew2", ["wd"] + [f"xcbf{j}" for j in range(16)] +
           [f"fb0_{i}" for i in range(8)] +
           ["fb16384_0", "fb16384_1", "fb20480_0", "fb20480_1", "vctok"])
    for htl in range(8):
        P.join(f"l1hh{htl}", ["ew2"])

    def emit_l1_ffn_expert(e):
        if e < 0:
            w1s, w3s, w2s = sw1_src, sw3_src, sw2_src
        else:
            w1s, w3s, w2s = ew1_src[e], ew3_src[e], ew2_src[e]
        P.dma("gpsimd", FB[:, 0:8192].rearrange("p (f d) -> p f d", d=1024),
              w2s, reads=["ew2"], writes=["ew2"])
        if e >= 0:
            P.dma("sync", SCR[0:1, 0:2048], SCR2[e:e + 1, 0:2048],
                  reads=["combT"], writes=["combrow", "SCR", "combbc"])
            for half in range(2):
                for c2 in range(2):
                    def b_cb2(t, half=half, c2=c2):
                        return t.matmul(PB[:, c2 * 512:(c2 + 1) * 512], ONESR_F,
                                        SCR[0:1, half * 1024 + c2 * 512:
                                            half * 1024 + (c2 + 1) * 512],
                                        start=True, stop=True)
                    P.op("tensor", b_cb2, reads=["combrow", "CONST2"],
                         writes=["PB"], pe_chain=True, inc=(c2 == 1))
                def b_cbe2(v, half=half):
                    return v.tensor_copy(SCR[:, half * 1024:(half + 1) * 1024],
                                         PB[:, 0:1024])
                P.op("vector", b_cbe2, reads=["PB"], writes=["SCR", "combbc"])
        base = pos[0]
        def load_pair(ci):
            wchunk_dma(pos[0], w1s[:, :, ci * 256:(ci + 1) * 256], 256,
                       "l1w1", nslots=4)
            pos[0] += 1
            wchunk_dma(pos[0], w3s[:, :, ci * 256:(ci + 1) * 256], 256,
                       "l1w3", nslots=4)
            pos[0] += 1
        load_pair(0)
        for ht in range(8):
            ci = ht // 2
            if ht % 2 == 0 and ci + 1 < 4:
                load_pair(ci + 1)
            s1 = (base + 2 * ci) % 4
            s3 = (base + 2 * ci + 1) % 4
            hofs = (ht % 2) * 128
            for tbb in range(2):
                for c2 in range(2):
                    for dt in range(ND):
                        def b_G(t, s1=s1, hofs=hofs, tbb=tbb, c2=c2, dt=dt):
                            return t.matmul(
                                PA[:, c2 * 512:(c2 + 1) * 512],
                                WBUF[:, s1 * 2048 + dt * 256 + hofs:
                                     s1 * 2048 + dt * 256 + hofs + 128],
                                BFX[:, dt * S + tbb * 1024 + c2 * 512:
                                    dt * S + tbb * 1024 + (c2 + 1) * 512],
                                start=(dt == 0), stop=(dt == ND - 1))
                        P.op("tensor", b_G, reads=[f"w{s1}", "l1w1", "BFX"],
                             writes=["PAg1"], pe_chain=True, inc=(dt == ND - 1))
                for c2 in range(2):
                    for dt in range(ND):
                        def b_U(t, s3=s3, hofs=hofs, tbb=tbb, c2=c2, dt=dt):
                            return t.matmul(
                                PA[:, 1024 + c2 * 512:1024 + (c2 + 1) * 512],
                                WBUF[:, s3 * 2048 + dt * 256 + hofs:
                                     s3 * 2048 + dt * 256 + hofs + 128],
                                BFX[:, dt * S + tbb * 1024 + c2 * 512:
                                    dt * S + tbb * 1024 + (c2 + 1) * 512],
                                start=(dt == 0), stop=(dt == ND - 1))
                        P.op("tensor", b_U, reads=[f"w{s3}", "l1w3", "BFX"],
                             writes=["PAu1"], pe_chain=True, inc=(dt == ND - 1))
                hcol = 8192 + ht * 2048 + tbb * 1024
                def b_sl(s_, hcol=hcol):
                    return s_.activation(FB[:, hcol:hcol + 1024],
                                         PA[:, 0:1024], AF.Silu)
                P.op("scalar", b_sl, reads=["PAg1"],
                     writes=[f"l1hh{ht}"])
                def b_hm(v, hcol=hcol):
                    return v.tensor_mul(FB[:, hcol:hcol + 1024],
                                        FB[:, hcol:hcol + 1024],
                                        PA[:, 1024:2048])
                P.op("vector", b_hm, reads=[f"l1hh{ht}", "PAu1"],
                     writes=[f"l1hh{ht}"])
        for dt in range(ND):
            for c4 in range(4):
                for ht in range(8):
                    def b_eo2(t, dt=dt, c4=c4, ht=ht):
                        return t.matmul(
                            PA[:, c4 * 512:(c4 + 1) * 512],
                            FB[:, ht * 1024 + dt * 128:ht * 1024 + (dt + 1) * 128],
                            FB[:, 8192 + ht * 2048 + c4 * 512:
                               8192 + ht * 2048 + (c4 + 1) * 512],
                            start=(ht == 0), stop=(ht == 7))
                    P.op("tensor", b_eo2,
                         reads=["ew2", f"l1hh{ht}"], writes=["PA"],
                         pe_chain=True, inc=(ht == 7))
            if e < 0:
                def b_acc2(v, dt=dt):
                    return v.tensor_add(RES[:, dt * S:(dt + 1) * S],
                                        RES[:, dt * S:(dt + 1) * S],
                                        PA[:, 0:2048])
                P.op("vector", b_acc2, reads=["RES", "PA"], writes=["RES"])
            else:
                for c4 in range(4):
                    def b_cm2(v, c4=c4):
                        return v.tensor_mul(SMALL[:, 0:512],
                                            PA[:, c4 * 512:(c4 + 1) * 512],
                                            SCR[:, c4 * 512:(c4 + 1) * 512])
                    P.op("vector", b_cm2, reads=["PA", "combbc"],
                         writes=["SMALL"])
                    def b_ac2(v, dt=dt, c4=c4):
                        return v.tensor_add(
                            RES[:, dt * S + c4 * 512:dt * S + (c4 + 1) * 512],
                            RES[:, dt * S + c4 * 512:dt * S + (c4 + 1) * 512],
                            SMALL[:, 0:512])
                    P.op("vector", b_ac2, reads=["RES", "SMALL"],
                         writes=["RES"])

    if not _DBG.get("skip_l1shared"):
        emit_l1_ffn_expert(-1)
    if not _DBG.get("skip_l1moe"):
        for e in range(_DBG.get("n_l1_experts", E0)):
            emit_l1_ffn_expert(e)

    if "dbg_x5a" in tap_aps:
        P.dma("gpsimd", tap_aps["dbg_x5a"], RES[:, 0:2048], reads=["RES"])
    if "dbg_l1comb" in tap_aps:
        P.dma("gpsimd", tap_aps["dbg_l1comb"], SCR2[:, 0:2048],
              reads=["combT"])
    for dt in range(ND):
        P.dma("sync", SCR[:, 0:2048],
              dram["x3scr"][:, dt * S:(dt + 1) * S],
              reads=["x3scr"], writes=["SCR", "x3back", "combbc", "combrow"])
        def b_x3a(v, dt=dt):
            return v.tensor_add(RES[:, dt * S:(dt + 1) * S],
                                RES[:, dt * S:(dt + 1) * S], SCR[:, 0:2048])
        P.op("vector", b_x3a, reads=["RES", "SCR", "x3back"], writes=["RES"])
    _ln_feature_major(nc, P, sbufs, EPS_LN, with_mean=True, normalize=True)
    P.barrier()
    _flip_res(nc, P, RES, PA, IDF, tok_to_feat=False)
    op_t = out.rearrange("(t p) d -> t p d", p=128)
    for it in range(NT):
        P.dma("sync", op_t[it], RES[:, it * D:(it + 1) * D], reads=["RES"])
    return


def _flip_res(nc, P, RES, PA, IDF, tok_to_feat=True):
    if tok_to_feat:
        def perm(k):
            dt, it = divmod(k, 16)
            return it * 8 + dt
    else:
        def perm(k):
            it, dt = divmod(k, 8)
            return dt * 16 + it
    P.fork("RES", [f"RESc{i}" for i in range(128)])
    seen = set()
    bank = [0]
    def tr(dstbank, src_chunk):
        def b(t, dstbank=dstbank, src_chunk=src_chunk):
            return t.transpose(PA[:, dstbank * 512:dstbank * 512 + 128],
                               RES[:, src_chunk * 128:(src_chunk + 1) * 128],
                               IDF)
        P.op("tensor", b, reads=[f"RESc{src_chunk}", "CONST2"],
             writes=[f"paB{dstbank}"])
    def cp(dstchunk, srcbank):
        def b(v, dstchunk=dstchunk, srcbank=srcbank):
            return v.tensor_copy(RES[:, dstchunk * 128:(dstchunk + 1) * 128],
                                 PA[:, srcbank * 512:srcbank * 512 + 128])
        P.op("vector", b, reads=[f"paB{srcbank}"], writes=[f"RESc{dstchunk}"])
    for k0 in range(128):
        if k0 in seen:
            continue
        cyc = [k0]
        k = perm(k0)
        while k != k0:
            cyc.append(k)
            k = perm(k)
        seen.update(cyc)
        if len(cyc) == 1:
            b = bank[0]; bank[0] = (bank[0] + 1) % 3
            tr(b, k0)
            cp(k0, b)
            continue
        tr(3, cyc[0])
        for i in range(len(cyc) - 1):
            b = bank[0]; bank[0] = (bank[0] + 1) % 3
            tr(b, cyc[i + 1])
            cp(cyc[i], b)
        cp(cyc[-1], 3)
    P.join("RES", [f"RESc{i}" for i in range(128)])


def _top2(nc, P, sbufs, ltok_key, mode="renorm"):
    SCR2 = sbufs["SCR2"]; SMALL = sbufs["SMALL"]
    if mode == "soft":
        def b_m1s(v):
            return v.reduce_max(SCR2[:, 128:144],
                                SCR2[:, 0:128].rearrange("p (t e) -> p t e", e=8),
                                axis=AXX)
        P.op("vector", b_m1s, reads=[ltok_key], writes=["t2m1", "t2scr"])
        def b_negm(v):
            return v.tensor_scalar_mul(SCR2[:, 144:160], SCR2[:, 128:144], -1.0)
        P.op("vector", b_negm, reads=["t2m1"], writes=["t2negm"])
        for it in range(NT):
            def b_es(s_, it=it):
                return s_.activation(SCR2[:, it * 8:(it + 1) * 8],
                                     SCR2[:, it * 8:(it + 1) * 8], AF.Exp,
                                     bias=SCR2[:, 144 + it:145 + it],
                                     accum_out=SCR2[:, 160 + it:161 + it])
            P.op("scalar", b_es, reads=[ltok_key, "t2negm"],
                 writes=[ltok_key, "t2z"])
            def b_est(s_, it=it):
                return s_.copy(SCR2[:, 160 + it:161 + it],
                               SCR2[:, 160 + it:161 + it])
            P.op("scalar", b_est, reads=["t2z"], writes=["t2z"])
        def b_rz(v):
            return v.reciprocal(SCR2[:, 176:192], SCR2[:, 160:176])
        P.op("vector", b_rz, reads=["t2z"], writes=["t2rz"])
        for it in range(NT):
            def b_pn(v, it=it):
                return v.tensor_scalar_mul(SCR2[:, it * 8:(it + 1) * 8],
                                           SCR2[:, it * 8:(it + 1) * 8],
                                           SCR2[:, 176 + it:177 + it])
            P.op("vector", b_pn, reads=[ltok_key, "t2rz"], writes=[ltok_key])
    def b_m1(v):
        return v.reduce_max(SCR2[:, 128:144],
                            SCR2[:, 0:128].rearrange("p (t e) -> p t e", e=8),
                            axis=AXX)
    P.op("vector", b_m1, reads=[ltok_key], writes=["t2m1", "t2scr"])
    for it in range(NT):
        def b_eq1(v, it=it):
            return v.tensor_scalar(SCR2[:, 224 + it * 8:232 + it * 8],
                                   SCR2[:, it * 8:(it + 1) * 8],
                                   SCR2[:, 128 + it:129 + it], None,
                                   ALU.is_ge)
        P.op("vector", b_eq1, reads=[ltok_key, "t2m1"], writes=["t2eq1"])
        def b_l2(v, it=it):
            return v.scalar_tensor_tensor(SCR2[:, 352 + it * 8:360 + it * 8],
                                          SCR2[:, 224 + it * 8:232 + it * 8],
                                          -1e30,
                                          SCR2[:, it * 8:(it + 1) * 8],
                                          ALU.mult, ALU.add)
        P.op("vector", b_l2, reads=["t2eq1", ltok_key], writes=["t2l2"])
    def b_m2(v):
        return v.reduce_max(SCR2[:, 144:160],
                            SCR2[:, 352:480].rearrange("p (t e) -> p t e", e=8),
                            axis=AXX)
    P.op("vector", b_m2, reads=["t2l2"], writes=["t2m2"])
    def b_dm(v):
        return v.tensor_sub(SCR2[:, 160:176], SCR2[:, 144:160],
                            SCR2[:, 128:144])
    P.op("vector", b_dm, reads=["t2m1", "t2m2"], writes=["t2dm"])
    if mode == "soft":
        def b_sg(s_):
            return s_.activation(SMALL[:, 32:48], SCR2[:, 160:176], AF.Sigmoid)
        P.op("scalar", b_sg, reads=["t2dm"], writes=["SMALL"])
        def b_w1s(v):
            return v.tensor_scalar(SMALL[:, 16:32], SMALL[:, 32:48], -1.0, 1.0,
                                   ALU.mult, ALU.add)
        P.op("vector", b_w1s, reads=["SMALL"], writes=["SMALL"])
    else:
        def b_ed(s_):
            return s_.activation(SCR2[:, 160:176], SCR2[:, 160:176], AF.Exp)
        P.op("scalar", b_ed, reads=["t2dm"], writes=["t2dm"])
        def b_w1a(v):
            return v.tensor_scalar_add(SMALL[:, 0:16], SCR2[:, 160:176], 1.0)
        P.op("vector", b_w1a, reads=["t2dm"], writes=["SMALL"])
        def b_w1(v):
            return v.reciprocal(SMALL[:, 16:32], SMALL[:, 0:16])
        P.op("vector", b_w1, reads=["SMALL"], writes=["SMALL"])
        def b_w2(v):
            return v.tensor_mul(SMALL[:, 32:48], SCR2[:, 160:176],
                                SMALL[:, 16:32])
        P.op("vector", b_w2, reads=["SMALL", "t2dm"], writes=["SMALL"])
    for it in range(NT):
        def b_eq2(v, it=it):
            return v.tensor_scalar(SCR2[:, 480 + it * 8:488 + it * 8],
                                   SCR2[:, 352 + it * 8:360 + it * 8],
                                   SCR2[:, 144 + it:145 + it], None,
                                   ALU.is_ge)
        P.op("vector", b_eq2, reads=["t2l2", "t2m2"], writes=["t2eq2"])
        def b_e2w(v, it=it):
            return v.tensor_scalar_mul(SCR2[:, 480 + it * 8:488 + it * 8],
                                       SCR2[:, 480 + it * 8:488 + it * 8],
                                       SMALL[:, 32 + it:33 + it])
        P.op("vector", b_e2w, reads=["t2eq2", "SMALL"], writes=["t2eq2"])
        def b_comb(v, it=it):
            return v.scalar_tensor_tensor(SCR2[:, 608 + it * 8:616 + it * 8],
                                          SCR2[:, 224 + it * 8:232 + it * 8],
                                          SMALL[:, 16 + it:17 + it],
                                          SCR2[:, 480 + it * 8:488 + it * 8],
                                          ALU.mult, ALU.add)
        P.op("vector", b_comb, reads=["t2eq1", "t2eq2", "SMALL"],
             writes=["comb"])


def _ln_feature_major(nc, P, sbufs, eps, with_mean=True, normalize=True):
    RES = sbufs["RES"]; SCR = sbufs["SCR"]; SCR2 = sbufs["SCR2"]
    PA = sbufs["PA"]; PB = sbufs["PB"]
    CONST2 = sbufs["CONST2"]
    ONESC_F = CONST2[:, 128:129]
    ONESR_F = CONST2[0:1, 144:272]

    if with_mean:
        for dt in range(ND):
            for c4 in range(4):
                def b_sy(t, dt=dt, c4=c4):
                    return t.matmul(PA[0:1, c4 * 512:(c4 + 1) * 512], ONESC_F,
                                    RES[:, dt * S + c4 * 512:dt * S + (c4 + 1) * 512],
                                    start=(dt == 0), stop=(dt == ND - 1))
                P.op("tensor", b_sy, reads=["RES", "CONST2"], writes=["PA"],
                     pe_chain=True, inc=(dt == ND - 1 and c4 == 3))
        def b_mur(v):
            return v.tensor_scalar_mul(SCR2[0:1, 0:2048], PA[0:1, 0:2048],
                                       1.0 / D)
        P.op("vector", b_mur, reads=["PA"],
             writes=["murow", "combT", "SCR2a"])
    for dt in range(ND):
        def b_sq(s_, dt=dt):
            return s_.activation(SCR[:, 0:2048],
                                 RES[:, dt * S:(dt + 1) * S], AF.Square)
        P.op("scalar", b_sq, reads=["RES"], writes=["SCR"])
        for c4 in range(4):
            def b_ss(t, dt=dt, c4=c4):
                return t.matmul(PA[0:1, c4 * 512:(c4 + 1) * 512], ONESC_F,
                                SCR[:, c4 * 512:(c4 + 1) * 512],
                                start=(dt == 0), stop=(dt == ND - 1))
            P.op("tensor", b_ss, reads=["SCR", "CONST2"], writes=["PA"],
                 pe_chain=True, inc=(c4 == 3))
    def b_vt(v):
        return v.tensor_scalar(SCR[0:1, 0:2048], PA[0:1, 0:2048],
                               1.0 / D, eps, ALU.mult, ALU.add)
    P.op("vector", b_vt, reads=["PA"], writes=["SCR", "vrow"])
    if with_mean:
        def b_msq(v):
            return v.tensor_mul(PA[0:1, 0:2048], SCR2[0:1, 0:2048],
                                SCR2[0:1, 0:2048])
        P.op("vector", b_msq, reads=["murow"], writes=["PA"])
        def b_v2(v):
            return v.tensor_sub(SCR[0:1, 0:2048], SCR[0:1, 0:2048],
                                PA[0:1, 0:2048])
        P.op("vector", b_v2, reads=["vrow", "PA"], writes=["SCR", "vrow"])
    def b_sqr(s_):
        return s_.activation(SCR[0:1, 0:2048], SCR[0:1, 0:2048], AF.Sqrt)
    P.op("scalar", b_sqr, reads=["vrow"], writes=["vrow"])
    def b_rc(v):
        return v.reciprocal(SCR[0:1, 0:2048], SCR[0:1, 0:2048])
    P.op("vector", b_rc, reads=["vrow"], writes=["vrow"])

    def bcast(row_ap, row_key, dst, dst_keys):
        for half in range(2):
            for c2 in range(2):
                def b_b(t, half=half, c2=c2):
                    return t.matmul(PB[:, c2 * 512:(c2 + 1) * 512], ONESR_F,
                                    row_ap[0:1, half * 1024 + c2 * 512:
                                           half * 1024 + (c2 + 1) * 512],
                                    start=True, stop=True)
                P.op("tensor", b_b, reads=[row_key, "CONST2"], writes=["PB"],
                     pe_chain=True, inc=(c2 == 1))
            def b_be(v, half=half, dst=dst):
                return v.tensor_copy(dst[:, half * 1024:(half + 1) * 1024],
                                     PB[:, 0:1024])
            P.op("vector", b_be, reads=["PB"], writes=list(dst_keys))
    bcast(SCR, "vrow", SCR, ["SCR", "rstdbc"])
    if with_mean:
        bcast(SCR2, "murow", SCR2, ["SCR2a", "murow", "mubc"])
    if not normalize:
        return
    for dt in range(ND):
        if with_mean:
            def b_n1(v, dt=dt):
                return v.tensor_sub(RES[:, dt * S:(dt + 1) * S],
                                    RES[:, dt * S:(dt + 1) * S],
                                    SCR2[:, 0:2048])
            P.op("vector", b_n1, reads=["RES", "mubc"], writes=["RES"])
        def b_n2(v, dt=dt):
            return v.tensor_mul(RES[:, dt * S:(dt + 1) * S],
                                RES[:, dt * S:(dt + 1) * S],
                                SCR[:, 0:2048])
        P.op("vector", b_n2, reads=["RES", "rstdbc"], writes=["RES"])


def _ln_token_major(P, RES, SCR, SCR2, SU0, SQ0, MU0, RS0, eps):
    for it in range(NT):
        def b_su(v, it=it):
            return v.reduce_sum(SCR2[:, SU0 + it:SU0 + it + 1],
                                RES[:, it * D:(it + 1) * D], axis=AXX)
        P.op("vector", b_su, reads=["RES"], writes=["lnsu"])
        def b_sq2(s, it=it):
            return s.activation(SCR[:, 0:1024], RES[:, it * D:(it + 1) * D],
                                AF.Square,
                                accum_out=SCR2[:, SQ0 + it:SQ0 + it + 1])
        P.op("scalar", b_sq2, reads=["RES"], writes=["SCR", "lnsq"])
        def b_sq2t(s_, it=it):
            return s_.copy(SCR2[:, SQ0 + it:SQ0 + it + 1],
                           SCR2[:, SQ0 + it:SQ0 + it + 1])
        P.op("scalar", b_sq2t, reads=["lnsq"], writes=["lnsq"])
    def b_mu(v):
        return v.tensor_scalar_mul(SCR2[:, MU0:MU0 + 16], SCR2[:, SU0:SU0 + 16],
                                   1.0 / D)
    P.op("vector", b_mu, reads=["lnsu"], writes=["lnmu"])
    def b_var(v):
        return v.tensor_scalar(SCR2[:, RS0:RS0 + 16], SCR2[:, SQ0:SQ0 + 16],
                               1.0 / D, eps, ALU.mult, ALU.add)
    P.op("vector", b_var, reads=["lnsq"], writes=["lnrs"])
    def b_musq(v):
        return v.tensor_mul(SCR[:, 0:16], SCR2[:, MU0:MU0 + 16],
                            SCR2[:, MU0:MU0 + 16])
    P.op("vector", b_musq, reads=["lnmu"], writes=["SCR"])
    def b_var3(v):
        return v.tensor_sub(SCR2[:, RS0:RS0 + 16], SCR2[:, RS0:RS0 + 16],
                            SCR[:, 0:16])
    P.op("vector", b_var3, reads=["SCR", "lnrs"], writes=["lnrs"])
    def b_sqrt(s):
        return s.activation(SCR2[:, RS0:RS0 + 16], SCR2[:, RS0:RS0 + 16], AF.Sqrt)
    P.op("scalar", b_sqrt, reads=["lnrs"], writes=["lnrs"])
    def b_rc2(v):
        return v.reciprocal(SCR2[:, RS0:RS0 + 16], SCR2[:, RS0:RS0 + 16])
    P.op("vector", b_rc2, reads=["lnrs"], writes=["lnrs"])
    for it in range(NT):
        def b_n(v, it=it):
            return v.tensor_scalar(RES[:, it * D:(it + 1) * D],
                                   RES[:, it * D:(it + 1) * D],
                                   SCR2[:, MU0 + it:MU0 + it + 1],
                                   SCR2[:, RS0 + it:RS0 + it + 1],
                                   ALU.subtract, ALU.mult)
        P.op("vector", b_n, reads=["RES", "lnmu", "lnrs"], writes=["RES"])


def _prep_inputs(inputs):
    T = lambda a: np.ascontiguousarray(np.swapaxes(np.asarray(a), -1, -2))
    d = {}
    d["l0_gate_wT"] = T(inputs["L0_ffn_gate_w"])
    d["l0_gate_b"] = np.asarray(inputs["L0_ffn_gate_b"]).reshape(E0, 1)
    d["l0_wgT"] = T(inputs["L0_ffn_gate_proj"])
    d["l0_wuT"] = T(inputs["L0_ffn_up_proj"])
    d["l0_wdT"] = T(inputs["L0_ffn_down_proj"])
    w_in = np.asarray(inputs["L1_attn_w_in"])
    d["l1_winT"] = w_in[0:2064].T
    d["l1_bin"] = np.asarray(inputs["L1_attn_b_in"])[0:2064].reshape(2064, 1)
    d["l1_wconv"] = np.asarray(inputs["L1_attn_w_conv"]).reshape(2048, 4)
    d["l1_bconv"] = np.asarray(inputs["L1_attn_b_conv"]).reshape(2048, 1)
    wx = np.asarray(inputs["L1_attn_w_xproj"])
    wdt = np.asarray(inputs["L1_attn_w_dt"])
    x96 = np.zeros((96, 2048), np.float32)
    x96[0:16] = wx[0:16]
    x96[32:48] = wx[16:32]
    x96[64:80] = wdt
    d["l1_wx96T"] = x96.T
    d["l1_bdt"] = np.asarray(inputs["L1_attn_b_dt"]).reshape(16, 1)
    d["l1_wout16T"] = np.asarray(inputs["L1_attn_w_out"])[:, 0:16].T
    d["l1f_g1"] = np.asarray(inputs["L1_ffn_g1"]).reshape(D, 1)
    d["l1f_g2"] = np.asarray(inputs["L1_ffn_g2"]).reshape(D, 1)
    d["l1f_WqT"] = T(inputs["L1_ffn_Wq"])
    d["l1f_WkcT"] = T(inputs["L1_ffn_Wkc"])
    d["l1f_WvcT"] = T(inputs["L1_ffn_Wvc"])
    d["l1f_WoT"] = T(inputs["L1_ffn_Wo"])
    d["l1f_WrT"] = T(inputs["L1_ffn_Wr"])
    d["l1f_ebias"] = np.asarray(inputs["L1_ffn_expert_bias"]).reshape(E0, 1)
    d["l1f_sW1T"] = T(inputs["L1_ffn_sW1"])
    d["l1f_sW2T"] = T(inputs["L1_ffn_sW2"])
    d["l1f_sW3T"] = T(inputs["L1_ffn_sW3"])
    d["l1f_eW1T"] = T(inputs["L1_ffn_eW1"])
    d["l1f_eW2T"] = T(inputs["L1_ffn_eW2"])
    d["l1f_eW3T"] = T(inputs["L1_ffn_eW3"])
    return {k: np.ascontiguousarray(v, dtype=np.float32) for k, v in d.items()}


def kernel(**inputs):
    x = np.ascontiguousarray(np.asarray(inputs["L0_attn_x"], dtype=np.float32))
    weights = _prep_inputs(inputs)
    nc = build_program(upto=4)
    in_maps = []
    for c in range(8):
        m = {"xr": np.ascontiguousarray(np.roll(x, -c, axis=0))}
        m.update(weights)
        in_maps.append(m)
    res = run_bass_kernel_spmd(nc, in_maps, core_ids=list(range(8)))
    return np.stack([res.results[c]["out"] for c in range(8)], axis=0)


# version 07
# speedup vs baseline: 1.1578x; 1.1578x over previous
import numpy as np
from contextlib import ExitStack

import concourse.bass as bass
import concourse.mybir as mybir
from concourse.bass_utils import run_bass_kernel_spmd

F32 = mybir.dt.float32
BF16 = mybir.dt.bfloat16
AF = mybir.ActivationFunctionType
ALU = mybir.AluOpType
AXX = mybir.AxisListType.X

D = 1024
S = 2048
B = 8
NT = S // 128
ND = D // 128
H_FF0 = 2816
NH0 = H_FF0 // 128
E0 = 8
EPS_LN = 1e-5
EPS_RMS = 1e-6
PI = float(np.pi)


class Prog:

    ENGS = ("tensor", "vector", "scalar", "gpsimd", "sync")

    PSUM_BANKS = {
        "PA": (0, 1, 2, 3), "PAg": (0,), "PAu": (1,), "PAe0": (2,),
        "PAe1": (3,), "PAg1": (0, 1), "PAu1": (2, 3),
        "paB0": (0,), "paB1": (1,), "paB2": (2,), "paB3": (3,),
        "PB": (4, 5), "PBg": (4,), "PBu": (5,),
        "PC": (6, 7), "pcv0": (6,), "pcv1024": (7,),
        "sp2": (0, 1, 2, 3), "delta": (0, 1, 2, 3), "xc16": (0, 1, 2, 3),
        "hscan": (0, 1, 2, 3),
    }

    @classmethod
    def _xkeys(cls, keys):
        out = []
        for k in keys:
            b = cls.PSUM_BANKS.get(k)
            if b is None:
                out.append(k)
            else:
                out.extend(f"psb{i}" for i in b)
        return tuple(out)

    def __init__(self, nc, st):
        self.nc = nc
        self.sems = {e: st.enter_context(nc.semaphore(f"c_{e}")) for e in self.ENGS}
        self.dsems = {q: st.enter_context(nc.semaphore(f"d_{q}"))
                      for q in ("sync", "gpsimd", "scalar")}
        self.items = {e: [] for e in self.ENGS}
        self.inc_count = {e: 0 for e in self.ENGS}
        self.dma_count = {q: 0 for q in self.dsems}
        self.pending = {e: [] for e in self.ENGS}
        self.last_write = {}
        self.readers = {}
        self.synced = {e: {} for e in self.ENGS}
        self.nops = 0

    class _Pend:
        __slots__ = ("val", "who")
        def __init__(self):
            self.val = None
            self.who = "?"

    def _resolve(self, k, v, eng, pe_chain):
        if isinstance(v, Prog._Pend):
            if v.val is None:
                if k == ("c", eng) and eng == "tensor" and pe_chain:
                    return None
                raise RuntimeError(f"{eng} op depends on un-milestoned {k} op")
            return v.val
        return v

    def _gather(self, eng, reads, writes, pe_chain=False):
        need = {}
        def add(marks):
            for k, v in marks.items():
                if pe_chain and eng == "tensor" and k == ("c", "tensor"):
                    continue
                val = self._resolve(k, v, eng, pe_chain)
                if val is None:
                    continue
                if need.get(k, -1) < val:
                    need[k] = val
        for r in reads:
            add(self.last_write.get(r, {}))
        for w in writes:
            add(self.last_write.get(w, {}))
            add(self.readers.get(w, {}))
        waits = []
        for k, v in sorted(need.items(), key=str):
            if self.synced[eng].get(k, -1) >= v:
                continue
            waits.append(((self.sems if k[0] == "c" else self.dsems)[k[1]], v))
            self.synced[eng][k] = v
        return waits

    def _record_rw(self, key, mark, reads, writes):
        for w in writes:
            self.last_write[w] = {key: mark}
            self.readers[w] = {}
        for r in reads:
            self.readers.setdefault(r, {})[key] = mark

    def op(self, eng, builder, reads=(), writes=(), inc=True, pe_chain=False):
        reads = self._xkeys(reads); writes = self._xkeys(writes)
        waits = self._gather(eng, reads, writes, pe_chain)
        key = ("c", eng)
        if inc:
            self.inc_count[eng] += 1
            val = self.inc_count[eng]
            for p in self.pending[eng]:
                p.val = val
            self.pending[eng] = []
            mark = val
            incs = [(self.sems[eng], 1)]
        else:
            mark = Prog._Pend()
            self.pending[eng].append(mark)
            incs = []
        self._record_rw(key, mark, reads, writes)
        self.items[eng].append((waits, builder, incs))

    def dma(self, q, out_ap, in_ap, reads=(), writes=(), **kw):
        reads = self._xkeys(reads); writes = self._xkeys(writes)
        waits = self._gather(q, reads, writes)
        self.dma_count[q] += 1
        key = ("d", q)
        mark = 16 * self.dma_count[q]
        self._record_rw(key, mark, reads, writes)
        def builder(eng, out_ap=out_ap, in_ap=in_ap, kw=kw):
            return eng.dma_start(out=out_ap, in_=in_ap, **kw)
        self.items[q].append((waits, builder, [(self.dsems[q], 16)]))

    def barrier(self):
        for e in self.ENGS:
            assert not self.pending[e], f"barrier with pending marks on {e}"
        marks = [(("c", e), self.inc_count[e]) for e in self.ENGS
                 if self.inc_count[e] > 0]
        marks += [(("d", q), 16 * self.dma_count[q]) for q in self.dsems
                  if self.dma_count[q] > 0]
        for eng in self.ENGS:
            waits = []
            for k, v in marks:
                if k == ("c", eng):
                    continue
                if self.synced[eng].get(k, -1) >= v:
                    continue
                waits.append(((self.sems if k[0] == "c" else self.dsems)[k[1]], v))
                self.synced[eng][k] = v
            if waits:
                self.items[eng].append((waits, None, []))
        self.last_write.clear()
        self.readers.clear()

    def fork(self, src_key, dst_keys):
        m = self.last_write.get(src_key, {})
        r = self.readers.get(src_key, {})
        for d in dst_keys:
            self.last_write[d] = dict(m)
            self.readers[d] = dict(r)

    def join(self, dst_key, src_keys):
        lw, rd = {}, {}
        def mval(v):
            return v.val if isinstance(v, Prog._Pend) else v
        for s in src_keys:
            for k, v in self.last_write.get(s, {}).items():
                if k not in lw or (mval(lw[k]) or -1) < (mval(v) or 1 << 60):
                    lw[k] = v
            for k, v in self.readers.get(s, {}).items():
                if k not in rd or (mval(rd[k]) or -1) < (mval(v) or 1 << 60):
                    rd[k] = v
        self.last_write[dst_key] = lw
        self.readers[dst_key] = rd

    def finalize(self, block):
        for q, cnt in self.dma_count.items():
            if cnt:
                self.items[q].append(([(self.dsems[q], 16 * cnt)], None, []))
        def make_body(eng_name):
            def body(eng):
                for waits, builder, incs in self.items[eng_name]:
                    for sem, val in waits:
                        eng.wait_ge(sem, val)
                    if builder is not None:
                        inst = builder(eng)
                        for sem, dv in incs:
                            inst.then_inc(sem, dv)
            return body
        block.tensor(make_body("tensor"))
        block.vector(make_body("vector"))
        block.scalar(make_body("scalar"))
        block.gpsimd(make_body("gpsimd"))
        block.sync(make_body("sync"))


_DBG = {}

def build_program(upto=4, taps=()):
    nc = bass.Bass("TRN2", target_bir_lowering=False)
    taps = set(taps)

    dram = {}
    def din(name, shape):
        dram[name] = nc.dram_tensor(name, list(shape), F32, kind="ExternalInput").ap()
        return dram[name]

    din("xr", (B, S, D))
    if upto >= 2:
        din("l0_gate_wT", (D, E0))
        din("l0_gate_b", (E0, 1))
        din("l0_wgT", (E0, D, H_FF0))
        din("l0_wuT", (E0, D, H_FF0))
        din("l0_wdT", (E0, H_FF0, D))
    if upto >= 3:
        din("l1_winT", (D, 2064))
        din("l1_bin", (2064, 1))
        din("l1_wconv", (2048, 4))
        din("l1_bconv", (2048, 1))
        din("l1_wx96T", (2048, 96))
        din("l1_bdt", (16, 1))
        din("l1_wout16T", (16, D))
    if upto >= 4:
        din("l1f_g1", (D, 1))
        din("l1f_g2", (D, 1))
        din("l1f_WqT", (D, D))
        din("l1f_WkcT", (D, 256))
        din("l1f_WvcT", (D, 256))
        din("l1f_WoT", (D, D))
        din("l1f_WrT", (D, E0))
        din("l1f_ebias", (E0, 1))
        din("l1f_sW1T", (D, D))
        din("l1f_sW2T", (D, D))
        din("l1f_sW3T", (D, D))
        din("l1f_eW1T", (E0, D, D))
        din("l1f_eW2T", (E0, D, D))
        din("l1f_eW3T", (E0, D, D))

    out = nc.dram_tensor("out", [S, D], F32, kind="ExternalOutput").ap()
    if upto >= 4:
        dram["x3scr"] = nc.dram_tensor("x3scr", [128, ND * S], F32,
                                       kind="Internal").ap()
    tap_aps = {t: nc.dram_tensor("tap_" + t, [S, D], F32, kind="ExternalOutput").ap()
               for t in taps}
    for name in _DBG:
        tap_aps["dbg_" + name] = nc.dram_tensor(
            "dbg_" + name, [128, 2048], F32, kind="ExternalOutput").ap()

    st = ExitStack()
    sbufs = {}
    def sb(name, shape, dtype):
        sbufs[name] = st.enter_context(nc.sbuf_tensor(name, list(shape), dtype))
        return sbufs[name]
    def psum(name, shape, dtype):
        sbufs[name] = st.enter_context(nc.psum_tensor(name, list(shape), dtype))
        return sbufs[name]

    with st:
        sb("RES", [128, ND * S], F32)
        sb("FB", [128, 2 * ND * S], BF16)
        sb("BFX", [128, ND * S], BF16)
        sb("WBUF", [128, 12288], BF16)
        sb("SCR", [128, S + 16], F32)
        sb("SCR2", [128, S], F32)
        sb("SMALL", [128, 512], F32)
        sb("WBUF2", [128, 1024], BF16)
        sb("CONST", [128, 288], BF16)
        sb("CONST2", [128, 272], F32)
        sb("GWT", [128, 64], F32)
        sb("VECS", [128, 128], F32)
        psum("PA", [128, 2048], F32)
        psum("PB", [128, 1024], F32)
        psum("PC", [128, 2048], BF16)

        P = Prog(nc, st)
        global _LAST_P
        _LAST_P = P
        block = st.enter_context(nc.Block())
        emit_all(nc, P, dram, sbufs, out, tap_aps, upto)
        P.finalize(block)
    return nc


_LAST_P = None


def emit_all(nc, P, dram, sbufs, out, tap_aps, upto):
    RES = sbufs["RES"]; FB = sbufs["FB"]; BFX = sbufs["BFX"]
    WBUF = sbufs["WBUF"]; WBUF2 = sbufs["WBUF2"]
    SCR = sbufs["SCR"]; SCR2 = sbufs["SCR2"]
    SMALL = sbufs["SMALL"]; CONST = sbufs["CONST"]
    PA = sbufs["PA"]; PB = sbufs["PB"]; PC = sbufs["PC"]
    xr = dram["xr"]

    def b_memset(g):
        return g.memset(CONST[:, 0:128], 0.0)
    P.op("gpsimd", b_memset, writes=["CONST"])
    def b_diag(g):
        return g.affine_select(out=CONST[:, 0:128], in_=CONST[:, 0:128],
                               compare_op=ALU.not_equal, fill=1.0, base=0,
                               pattern=[[-1, 128]], channel_multiplier=1)
    P.op("gpsimd", b_diag, reads=["CONST"], writes=["CONST"])
    def b_ones(g):
        return g.memset(CONST[:, 128:130], 1.0)
    P.op("gpsimd", b_ones, reads=["CONST"], writes=["CONST"])
    def b_onesr(g):
        return g.memset(CONST[0:1, 160:288], 1.0)
    P.op("gpsimd", b_onesr, reads=["CONST"], writes=["CONST"])
    IDENT = CONST[:, 0:128]
    ONESC = CONST[:, 128:129]
    ONESR = CONST[0:1, 160:288]
    CONST2 = sbufs["CONST2"]; GWT = sbufs["GWT"]; VECS = sbufs["VECS"]
    def b_c2z(g):
        return g.memset(CONST2[:, 0:128], 0.0)
    P.op("gpsimd", b_c2z, writes=["CONST2"])
    def b_c2d(g):
        return g.affine_select(out=CONST2[:, 0:128], in_=CONST2[:, 0:128],
                               compare_op=ALU.not_equal, fill=1.0, base=0,
                               pattern=[[-1, 128]], channel_multiplier=1)
    P.op("gpsimd", b_c2d, reads=["CONST2"], writes=["CONST2"])
    def b_c2o(g):
        return g.memset(CONST2[:, 128:130], 1.0)
    P.op("gpsimd", b_c2o, reads=["CONST2"], writes=["CONST2"])
    def b_c2r(g):
        return g.memset(CONST2[0:1, 144:272], 1.0)
    P.op("gpsimd", b_c2r, reads=["CONST2"], writes=["CONST2"])
    IDF = CONST2[:, 0:128]
    ONESC_F = CONST2[:, 128:129]
    ONESR_F = CONST2[0:1, 144:272]

    x_src = xr.rearrange("b (t p) d -> b t p d", p=128)

    P.dma("sync", RES.rearrange("p (t d) -> p t d", d=D),
          xr.rearrange("b (t p) d -> b p t d", p=128)[0], writes=["RES"])
    def b_cast(v):
        return v.tensor_copy(BFX[:, :], RES[:, :])
    P.op("vector", b_cast, reads=["RES"], writes=["BFX"])

    def emit_fold(src_ap, src_keys, mt, first):
        def b1(v, src_ap=src_ap):
            return v.tensor_add(SMALL[:, 0:512], src_ap[:, 0:512], src_ap[:, 512:1024])
        P.op("vector", b1, reads=list(src_keys), writes=["SMALL"])
        def b2(v):
            return v.tensor_add(SMALL[:, 0:256], SMALL[:, 0:256], SMALL[:, 256:512])
        P.op("vector", b2, reads=["SMALL"], writes=["SMALL"])
        def b3(v):
            return v.tensor_add(SMALL[:, 0:128], SMALL[:, 0:128], SMALL[:, 128:256])
        P.op("vector", b3, reads=["SMALL"], writes=["SMALL"])
        mcol = SCR2[:, mt * 64:(mt + 1) * 64]
        if first:
            def b4(v, mcol=mcol):
                return v.tensor_add(mcol, SMALL[:, 0:64], SMALL[:, 64:128])
            P.op("vector", b4, reads=["SMALL"], writes=["m_acc"], inc=True)
        else:
            def b4(v):
                return v.tensor_add(SMALL[:, 0:64], SMALL[:, 0:64], SMALL[:, 64:128])
            P.op("vector", b4, reads=["SMALL"], writes=["SMALL"])
            def b5(v, mcol=mcol):
                return v.tensor_add(mcol, mcol, SMALL[:, 0:64])
            P.op("vector", b5, reads=["SMALL", "m_acc"], writes=["m_acc"], inc=True)

    for mt in range(NT):
        emit_fold(RES[:, mt * D:(mt + 1) * D], ["RES"], mt, first=True)
    for b in range(1, B):
        for mt in range(NT):
            slot = (b * NT + mt) % 2
            q = "sync" if slot == 0 else "scalar"
            dst = SCR[:, slot * 1024:(slot + 1) * 1024]
            P.dma(q, dst, x_src[b, mt], writes=[f"xs{slot}"])
            emit_fold(dst, [f"xs{slot}"], mt, first=False)

    def b_scale(v):
        return v.tensor_scalar_mul(SCR2[:, 0:1024], SCR2[:, 0:1024], 1.0 / 128.0)
    P.op("vector", b_scale, reads=["m_acc"], writes=["m_acc"])
    def b_mcast(v):
        return v.tensor_copy(WBUF[:, 0:1024], SCR2[:, 0:1024])
    P.op("vector", b_mcast, reads=["m_acc"], writes=["WBUFh"])
    for mt in range(NT):
        def b_tr(t, mt=mt):
            return t.transpose(PC[0:64, mt * 128:(mt + 1) * 128],
                               WBUF[:, mt * 64:(mt + 1) * 64], IDENT)
        P.op("tensor", b_tr, reads=["WBUFh", "CONST"], writes=["PC"],
             inc=(mt == NT - 1), pe_chain=True)
    def b_mTc(v):
        return v.tensor_copy(WBUF[0:64, 1024:3072], PC[0:64, 0:2048])
    P.op("vector", b_mTc, reads=["PC"], writes=["WBUFh"])
    MT_ = WBUF[0:64, 1024:3072]

    for kt in range(NT):
        for half in range(2):
            def b_s(t, kt=kt, half=half):
                return t.matmul(PB[0:1, half * 512:(half + 1) * 512], ONESC,
                                BFX[:, kt * D + half * 512:kt * D + (half + 1) * 512],
                                start=(kt == 0), stop=(kt == NT - 1))
            P.op("tensor", b_s, reads=["BFX", "CONST"], writes=["PB"],
                 inc=(kt == NT - 1 and half == 1), pe_chain=True)
    def b_sev(v):
        return v.tensor_copy(SCR[0:1, 0:1024], PB[0:1, 0:1024])
    P.op("vector", b_sev, reads=["PB"], writes=["SCR", "Srow", "xs0", "xs1"])
    def b_sbf(v):
        return v.tensor_copy(WBUF[0:1, 3072:4096], SCR[0:1, 0:1024])
    P.op("vector", b_sbf, reads=["SCR", "Srow"], writes=["WBUFh"])
    SROW_BF = WBUF[0:1, 3072:4096]

    for it in range(NT):
        for ncol in range(4):
            def b_mm(t, it=it, ncol=ncol):
                return t.matmul(PA[:, ncol * 512:(ncol + 1) * 512],
                                MT_[:, it * 128:(it + 1) * 128],
                                MT_[:, ncol * 512:(ncol + 1) * 512],
                                start=True, stop=True)
            P.op("tensor", b_mm, reads=["WBUFh"], writes=["PA"],
                 inc=(ncol == 3), pe_chain=True)
        def b_clip(v):
            return v.tensor_scalar(SCR[:, 0:2048], PA[:, :], PI, -PI, ALU.min, ALU.max)
        P.op("vector", b_clip, reads=["PA"], writes=["SCR"])
        def b_sin(s):
            return s.activation(SCR[:, 0:2048], SCR[:, 0:2048], AF.Sin, scale=0.5)
        P.op("scalar", b_sin, reads=["SCR"], writes=["SCR"])
        def b_sq(s, it=it):
            return s.activation(FB[:, it * 2048:(it + 1) * 2048], SCR[:, 0:2048],
                                AF.Square)
        P.op("scalar", b_sq, reads=["SCR"], writes=["FB"])
    for it in range(NT):
        def b_exp(s, it=it):
            return s.activation(SCR[:, 0:2048], FB[:, it * 2048:(it + 1) * 2048],
                                AF.Exp,
                                accum_out=SCR2[:, 1024 + it:1024 + it + 1])
        P.op("scalar", b_exp, reads=["FB"], writes=["SCR", "Dcol"])
        def b_expt(s_, it=it):
            return s_.copy(SCR2[:, 1024 + it:1024 + it + 1],
                           SCR2[:, 1024 + it:1024 + it + 1])
        P.op("scalar", b_expt, reads=["Dcol"], writes=["Dcol"])
        def b_f(v, it=it):
            return v.tensor_scalar_add(FB[:, it * 2048:(it + 1) * 2048],
                                       SCR[:, 0:2048], -1.0)
        P.op("vector", b_f, reads=["SCR"], writes=["FB"])
    def b_rcp(v):
        return v.reciprocal(SCR2[:, 1040:1056], SCR2[:, 1024:1040])
    P.op("vector", b_rcp, reads=["Dcol"], writes=["Drec"])
    if "dbg_scr2" in tap_aps:
        P.dma("gpsimd", tap_aps["dbg_scr2"], SCR2[:, 0:2048],
              reads=["m_acc", "Dcol", "Drec"])
    if "dbg_F0" in tap_aps:
        P.dma("gpsimd", tap_aps["dbg_F0"], FB[:, 0:2048], reads=["FB"])
    if "dbg_F5" in tap_aps:
        P.dma("gpsimd", tap_aps["dbg_F5"], FB[:, 5*2048:6*2048], reads=["FB"])

    for it in range(NT):
        for half in range(2):
            def b_s0(t, half=half):
                return t.matmul(PB[:, half * 512:(half + 1) * 512], ONESR,
                                SROW_BF[:, half * 512:(half + 1) * 512],
                                start=True, stop=False)
            P.op("tensor", b_s0, reads=["WBUFh", "CONST"], writes=["PB"],
                 pe_chain=True, inc=False)
        for jt in range(NT):
            for half in range(2):
                def b_o(t, it=it, jt=jt, half=half):
                    return t.matmul(
                        PB[:, half * 512:(half + 1) * 512],
                        FB[:, jt * 2048 + it * 128:jt * 2048 + (it + 1) * 128],
                        BFX[:, jt * D + half * 512:jt * D + (half + 1) * 512],
                        start=False, stop=(jt == NT - 1))
                P.op("tensor", b_o, reads=["FB", "BFX"], writes=["PB"],
                     pe_chain=True, inc=(jt == NT - 1 and half == 1))
        def b_y1(v, it=it):
            return v.scalar_tensor_tensor(
                RES[:, it * D:(it + 1) * D], PB[:, 0:1024],
                SCR2[:, 1040 + it:1041 + it],
                RES[:, it * D:(it + 1) * D], ALU.mult, ALU.add)
        P.op("vector", b_y1, reads=["PB", "Drec", "RES"], writes=["RES"])

    if "dbg_y1a" in tap_aps:
        P.dma("gpsimd", tap_aps["dbg_y1a"], RES[:, 0:2048], reads=["RES"])
    if "dbg_y1b" in tap_aps:
        P.dma("gpsimd", tap_aps["dbg_y1b"], RES[:, 4*1024:6*1024], reads=["RES"])
    SU0, SQ0, MU0, RS0 = 1056, 1072, 1088, 1104
    if not _DBG.get("skip_ln"):
        _ln_token_major(P, RES, SCR, SCR2, SU0, SQ0, MU0, RS0, EPS_LN)

    if "x1_tok" in tap_aps:
        tp = tap_aps["x1_tok"].rearrange("(t p) d -> t p d", p=128)
        for it in range(NT):
            P.dma("sync", tp[it], RES[:, it * D:(it + 1) * D], reads=["RES"])

    if "dbg_ln" in tap_aps:
        P.dma("gpsimd", tap_aps["dbg_ln"], SCR2[:, 0:2048],
              reads=["lnsu", "lnsq", "lnmu", "lnrs"])
    if upto == 1:
        op_t = out.rearrange("(t p) d -> t p d", p=128)
        for it in range(NT):
            P.dma("sync", op_t[it], RES[:, it * D:(it + 1) * D], reads=["RES"])
        return

    P.barrier()
    P.dma("sync", GWT[:, 0:64].rearrange("p (t e) -> p t e", e=8),
          dram["l0_gate_wT"].rearrange("(t p) e -> p t e", p=128),
          writes=["GWT"])
    P.dma("sync", VECS[0:8, 0:1], dram["l0_gate_b"], writes=["VECS0"])

    _flip_res(nc, P, RES, PA, IDF, tok_to_feat=True)
    def b_cast2(v):
        return v.tensor_copy(BFX[:, :], RES[:, :])
    P.op("vector", b_cast2, reads=["RES"], writes=["BFX"])

    for dt in range(ND):
        for c4 in range(4):
            def b_lg(t, dt=dt, c4=c4):
                return t.matmul(PA[0:8, c4 * 512:(c4 + 1) * 512],
                                GWT[:, dt * 8:(dt + 1) * 8],
                                RES[:, dt * S + c4 * 512:dt * S + (c4 + 1) * 512],
                                start=(dt == 0), stop=(dt == ND - 1))
            P.op("tensor", b_lg, reads=["GWT", "RES"], writes=["PA"],
                 pe_chain=True, inc=(dt == ND - 1 and c4 == 3))
    def b_lgev(v):
        return v.tensor_scalar_add(SCR[0:8, 0:2048], PA[0:8, 0:2048],
                                   VECS[0:8, 0:1])
    P.op("vector", b_lgev, reads=["PA", "VECS0"], writes=["SCR", "lgT"])
    for it in range(NT):
        def b_lt(t, it=it):
            return t.transpose(PB[:, it * 8:(it + 1) * 8],
                               SCR[0:8, it * 128:(it + 1) * 128], IDF[0:8, 0:8])
        P.op("tensor", b_lt, reads=["SCR", "CONST2"], writes=["PB"],
             pe_chain=True, inc=(it == NT - 1))
    def b_ltev(v):
        return v.tensor_copy(SCR2[:, 0:128], PB[:, 0:128])
    P.op("vector", b_ltev, reads=["PB"], writes=["ltok"])
    _top2(nc, P, sbufs, "ltok")
    for it in range(NT):
        def b_cT(t, it=it):
            return t.transpose(PA[0:8, it * 128:(it + 1) * 128],
                               SCR2[:, 608 + it * 8:616 + it * 8], IDF)
        P.op("tensor", b_cT, reads=["comb", "CONST2"], writes=["PA"],
             pe_chain=True, inc=(it == NT - 1))
    def b_cTev(v):
        return v.tensor_copy(SCR2[0:8, 0:2048], PA[0:8, 0:2048])
    P.op("vector", b_cTev, reads=["PA"],
         writes=["combT", "ltok", "comb", "t2scr"])

    if "dbg_comb" in tap_aps:
        P.dma("gpsimd", tap_aps["dbg_comb"], SCR2[:, 0:2048],
              reads=["comb", "combT", "t2m1", "t2m2", "ltok"])
    if "dbg_flip" in tap_aps:
        P.dma("gpsimd", tap_aps["dbg_flip"], RES[:, 0:2048], reads=["RES"])
    WD0 = 0
    HH0 = 22528
    wgT_src = dram["l0_wgT"].rearrange("e (t p) h -> e p t h", p=128)
    wuT_src = dram["l0_wuT"].rearrange("e (t p) h -> e p t h", p=128)
    wdT_src = dram["l0_wdT"].rearrange("e (c p) d -> e p c d", p=128)

    def hh_ap(ht):
        if ht < 20:
            return FB[:, HH0 + ht * 512:HH0 + (ht + 1) * 512]
        return WBUF2[:, (ht - 20) * 512:(ht - 19) * 512]

    NCH = 11
    P.join("wd", ["FB"])
    for ht in range(NH0):
        P.join(f"hh{ht}", ["FB"])
    for sk in ("w0", "w1", "w2", "w3", "w4", "w5"):
        P.join(sk, ["WBUFh"])
    n_exp = _DBG.get("n_experts", E0) if not _DBG.get("skip_experts") else 0
    for e in range(n_exp):
        P.dma("gpsimd", FB[:, 0:22528].rearrange("p (c d) -> p c d", d=1024),
              wdT_src[e], reads=(), writes=["wd"])
        P.dma("sync", SCR[0:1, 0:2048], SCR2[e:e + 1, 0:2048],
              reads=["combT"], writes=["combrow", "SCR", "combbc"])
        for half in range(2):
            for c2 in range(2):
                def b_cb(t, half=half, c2=c2):
                    return t.matmul(PB[:, c2 * 512:(c2 + 1) * 512], ONESR_F,
                                    SCR[0:1, half * 1024 + c2 * 512:
                                        half * 1024 + (c2 + 1) * 512],
                                    start=True, stop=True)
                P.op("tensor", b_cb, reads=["combrow", "CONST2"], writes=["PB"],
                     pe_chain=True, inc=(c2 == 1))
            def b_cbe(v, half=half):
                return v.tensor_copy(SCR[:, half * 1024:(half + 1) * 1024],
                                     PB[:, 0:1024])
            P.op("vector", b_cbe, reads=["PB"], writes=["SCR", "combbc"])

        def pair_dma(pos):
            ci = pos % NCH
            sg = (2 * pos) % 6
            su = (2 * pos + 1) % 6
            P.dma("gpsimd",
                  WBUF[:, sg * 2048:(sg + 1) * 2048].rearrange(
                      "p (t h) -> p t h", h=256),
                  wgT_src[e, :, :, ci * 256:(ci + 1) * 256],
                  writes=[f"w{sg}"])
            P.dma("gpsimd",
                  WBUF[:, su * 2048:(su + 1) * 2048].rearrange(
                      "p (t h) -> p t h", h=256),
                  wuT_src[e, :, :, ci * 256:(ci + 1) * 256],
                  writes=[f"w{su}"])
        pair_dma(0)
        pair_dma(1)
        for tb in range(4):
            t0 = tb * 512
            for ht in range(NH0):
                ci = ht // 2
                pos = tb * NCH + ci
                if ht % 2 == 0 and pos + 2 < 4 * NCH:
                    pair_dma(pos + 2)
                sg = (2 * pos) % 6
                su = (2 * pos + 1) % 6
                hofs = (ht % 2) * 128
                if ht % 2 == 0:
                    gps, gk = PA[:, 0:512], "PAg"
                    ups, uk = PA[:, 512:1024], "PAu"
                else:
                    gps, gk = PB[:, 0:512], "PBg"
                    ups, uk = PB[:, 512:1024], "PBu"
                for dt in range(ND):
                    def b_g(t, dt=dt, sg=sg, hofs=hofs, gps=gps, t0=t0):
                        return t.matmul(
                            gps,
                            WBUF[:, sg * 2048 + dt * 256 + hofs:
                                 sg * 2048 + dt * 256 + hofs + 128],
                            BFX[:, dt * S + t0:dt * S + t0 + 512],
                            start=(dt == 0), stop=(dt == ND - 1))
                    P.op("tensor", b_g, reads=[f"w{sg}", "BFX"], writes=[gk],
                         pe_chain=True, inc=(dt == ND - 1))
                for dt in range(ND):
                    def b_u(t, dt=dt, su=su, hofs=hofs, ups=ups, t0=t0):
                        return t.matmul(
                            ups,
                            WBUF[:, su * 2048 + dt * 256 + hofs:
                                 su * 2048 + dt * 256 + hofs + 128],
                            BFX[:, dt * S + t0:dt * S + t0 + 512],
                            start=(dt == 0), stop=(dt == ND - 1))
                    P.op("tensor", b_u, reads=[f"w{su}", "BFX"], writes=[uk],
                         pe_chain=True, inc=(dt == ND - 1))
                def b_silu(s_, gps=gps):
                    return s_.activation(SMALL[:, 0:512], gps, AF.Silu)
                P.op("scalar", b_silu, reads=[gk], writes=["SMALL"])
                def b_hh(v, ups=ups, ht=ht):
                    return v.tensor_mul(hh_ap(ht), SMALL[:, 0:512], ups)
                P.op("vector", b_hh, reads=["SMALL", uk], writes=[f"hh{ht}"])
            if e == 0 and tb == 0 and "dbg_hh" in tap_aps:
                P.dma("gpsimd", tap_aps["dbg_hh"], FB[:, HH0:HH0 + 2048],
                      reads=["hh0", "hh1", "hh2", "hh3"])
            if e == 0 and tb == 0 and "dbg_wg" in tap_aps:
                P.dma("gpsimd", tap_aps["dbg_wg"], WBUF[:, 0:2048],
                      reads=["wg0"])
            if e == 0 and tb == 0 and "dbg_wd" in tap_aps:
                P.dma("gpsimd", tap_aps["dbg_wd"], FB[:, 0:2048],
                      reads=["wd"])
            for dt in range(ND):
                eo, ek = (PA[:, 1024:1536], "PAe0") if dt % 2 == 0 else                          (PA[:, 1536:2048], "PAe1")
                for ht in range(NH0):
                    def b_eo(t, dt=dt, ht=ht, eo=eo):
                        return t.matmul(
                            eo, FB[:, ht * 1024 + dt * 128:ht * 1024 + dt * 128 + 128],
                            hh_ap(ht),
                            start=(ht == 0), stop=(ht == NH0 - 1))
                    P.op("tensor", b_eo, reads=["wd", f"hh{ht}"], writes=[ek],
                         pe_chain=True, inc=(ht == NH0 - 1))
                def b_cm(v, eo=eo, t0=t0):
                    return v.tensor_mul(SMALL[:, 0:512], eo,
                                        SCR[:, t0:t0 + 512])
                P.op("vector", b_cm, reads=[ek, "combbc"], writes=["SMALL"])
                def b_acc(v, dt=dt, t0=t0):
                    return v.tensor_add(RES[:, dt * S + t0:dt * S + t0 + 512],
                                        RES[:, dt * S + t0:dt * S + t0 + 512],
                                        SMALL[:, 0:512])
                P.op("vector", b_acc, reads=["RES", "SMALL"], writes=["RES"])

    if "dbg_y2" in tap_aps:
        P.dma("gpsimd", tap_aps["dbg_y2"], RES[:, 0:2048], reads=["RES"])
    if "dbg_y2b" in tap_aps:
        P.dma("gpsimd", tap_aps["dbg_y2b"], RES[:, 5 * 2048:6 * 2048],
              reads=["RES"])
    _ln_feature_major(nc, P, sbufs, EPS_LN, with_mean=True)

    if upto == 2:
        flat = out.rearrange("(dt p two) d -> dt p (two d)", p=128, two=2)
        for dt in range(ND):
            P.dma("sync", flat[dt], RES[:, dt * S:(dt + 1) * S], reads=["RES"])
        return

    P.barrier()
    binr = dram["l1_bin"].rearrange("(a one) o -> a (one o)", one=1)
    for ct in range(17):
        n = 128 if ct < 16 else 16
        P.dma("scalar", VECS[0:n, 1 + ct:2 + ct],
              dram["l1_bin"][ct * 128:ct * 128 + n, 0:1], writes=["VECSbin"])
    for ct in range(16):
        P.dma("scalar", VECS[:, 18 + ct:19 + ct],
              dram["l1_bconv"][ct * 128:(ct + 1) * 128, 0:1],
              writes=["VECSbconv"])
        P.dma("scalar", VECS[:, 34 + ct * 4:38 + ct * 4],
              dram["l1_wconv"][ct * 128:(ct + 1) * 128, :],
              writes=["VECSwconv"])
    P.dma("scalar", VECS[0:16, 98:99], dram["l1_bdt"], writes=["VECSbdt"])

    def b_cast3(v):
        return v.tensor_copy(BFX[:, :], RES[:, :])
    P.op("vector", b_cast3, reads=["RES"], writes=["BFX"])

    for k, old_ in (("win0", "w0"), ("win1", "w0"), ("x96w", "w1"),
                    ("wo16", "w1"), ("wo16", "w2"), ("y16bf", "w2"),
                    ("y16bf", "w3")):
        P.join(k, [k, old_] if k in P.last_write else [old_])
    for ct in range(16):
        P.join(f"xcbf{ct}", ["wd"] + [f"hh{h}" for h in range(20)])

    P.dma("gpsimd", WBUF[:, 2048:3584].rearrange("p (c m) -> p c m", m=96),
          dram["l1_wx96T"].rearrange("(c p) m -> p c m", p=128),
          reads=["x96w"], writes=["x96w"])
    P.dma("gpsimd", WBUF[0:16, 3584:4608], dram["l1_wout16T"],
          reads=["wo16"], writes=["wo16"])

    win_src = dram["l1_winT"].rearrange("(t p) c -> p t c", p=128)
    def win_dma(ct):
        slot = ct % 2
        n = 128 if ct < 16 else 16
        P.dma("gpsimd",
              WBUF[:, slot * 1024:slot * 1024 + 8 * n].rearrange(
                  "p (t c) -> p t c", c=n),
              win_src[:, :, ct * 128:ct * 128 + n],
              reads=[f"win{slot}"], writes=[f"win{slot}"])
    def b_pad(g):
        return g.memset(SCR[:, 0:3], 0.0)
    P.op("gpsimd", b_pad, reads=["SCR"], writes=["SCR", "scrpad"])

    win_dma(0)
    for ct in range(16):
        if ct + 1 < 16:
            win_dma(ct + 1)
        slot = ct % 2
        for c4 in range(4):
            for dt in range(ND):
                def b_xz(t, c4=c4, dt=dt, slot=slot):
                    return t.matmul(
                        PA[:, c4 * 512:(c4 + 1) * 512],
                        WBUF[:, slot * 1024 + dt * 128:slot * 1024 + (dt + 1) * 128],
                        BFX[:, dt * S + c4 * 512:dt * S + (c4 + 1) * 512],
                        start=(dt == 0), stop=(dt == ND - 1))
                P.op("tensor", b_xz, reads=[f"win{slot}", "BFX"], writes=["PA"],
                     pe_chain=True, inc=(dt == ND - 1))
        for c4 in range(4):
            def b_xe(v, ct=ct, c4=c4):
                return v.tensor_scalar_add(
                    SCR[:, 3 + c4 * 512:3 + (c4 + 1) * 512],
                    PA[:, c4 * 512:(c4 + 1) * 512],
                    VECS[:, 1 + ct:2 + ct])
            P.op("vector", b_xe, reads=[f"psb{c4}", "VECSbin", "scrpad"],
                 writes=["SCR", "xcpre"])
        def b_p1(s_, ct=ct):
            return s_.activation(SCR2[:, 0:2048], SCR[:, 1:2049], AF.Identity
                                 if False else AF.Copy,
                                 scale=VECS[:, 35 + ct * 4:36 + ct * 4])
        P.op("scalar", b_p1, reads=["SCR", "VECSwconv"],
             writes=["SCR2", "convacc"])
        for eng, k in (("vector", 0), ("vector", 2), ("vector", 3)):
            def b_tap(v, k=k, ct=ct):
                return v.scalar_tensor_tensor(
                    SCR2[:, 0:2048], SCR[:, k:k + 2048],
                    VECS[:, 34 + ct * 4 + k:35 + ct * 4 + k],
                    SCR2[:, 0:2048], ALU.mult, ALU.add)
            P.op(eng, b_tap, reads=["SCR", "VECSwconv", "convacc"],
                 writes=["SCR2", "convacc"])
        def b_silu3(s_, ct=ct):
            return s_.activation(FB[:, ct * 2048:(ct + 1) * 2048],
                                 SCR2[:, 0:2048], AF.Silu,
                                 bias=VECS[:, 18 + ct:19 + ct])
        P.op("scalar", b_silu3, reads=["SCR2", "convacc", "VECSbconv"],
             writes=[f"xcbf{ct}"])

    for half in range(2):
        for ct in range(16):
            for c2 in range(2):
                def b_x96(t, half=half, ct=ct, c2=c2):
                    return t.matmul(
                        PB[0:96, c2 * 512:(c2 + 1) * 512],
                        WBUF[:, 2048 + ct * 96:2048 + (ct + 1) * 96],
                        FB[:, ct * 2048 + half * 1024 + c2 * 512:
                           ct * 2048 + half * 1024 + (c2 + 1) * 512],
                        start=(ct == 0), stop=(ct == 15))
                P.op("tensor", b_x96, reads=["x96w", f"xcbf{ct}"], writes=["PB"],
                     pe_chain=True, inc=(ct == 15 and c2 == 1))
        def b_x96e(v, half=half):
            return v.tensor_copy(SCR2[0:96, half * 1024:(half + 1) * 1024],
                                 PB[0:96, 0:1024])
        P.op("vector", b_x96e, reads=["PB"],
             writes=["x96", "SCR2", "convacc"])
    def b_bdt(v):
        return v.tensor_scalar_add(SCR2[64:80, 0:2048], SCR2[64:80, 0:2048],
                                   VECS[0:16, 98:99])
    P.op("vector", b_bdt, reads=["x96", "VECSbdt"], writes=["x96"])

    if "dbg_xc0" in tap_aps:
        P.dma("gpsimd", tap_aps["dbg_xc0"], FB[:, 0:2048], reads=["xcbf0"])
    if "dbg_xc9" in tap_aps:
        P.dma("gpsimd", tap_aps["dbg_xc9"], FB[:, 9 * 2048:10 * 2048],
              reads=["xcbf9"])
    if "dbg_x96" in tap_aps:
        P.dma("gpsimd", tap_aps["dbg_x96"], SCR2[:, 0:2048], reads=["x96"])
    def b_abs(s_):
        return s_.activation(SCR[64:80, 0:2048], SCR2[64:80, 0:2048], AF.Abs)
    P.op("scalar", b_abs, reads=["x96"], writes=["SCR", "sp1"])
    def b_expn(s_):
        return s_.activation(SCR[64:80, 0:2048], SCR[64:80, 0:2048],
                             AF.Exp, scale=-1.0)
    P.op("scalar", b_expn, reads=["sp1"], writes=["sp1"])
    def b_ln1p(s_):
        return s_.activation(SCR[64:80, 0:2048], SCR[64:80, 0:2048],
                             AF.Ln, bias=1.0)
    P.op("scalar", b_ln1p, reads=["sp1"], writes=["sp1"])
    def b_relu(s_):
        return s_.activation(PA[0:16, 0:2048], SCR2[64:80, 0:2048], AF.Relu)
    P.op("scalar", b_relu, reads=["x96", "PA"], writes=["PA", "sp2"])
    def b_delta(v):
        return v.tensor_add(PA[64:80, 0:2048], SCR[64:80, 0:2048],
                            PA[0:16, 0:2048])
    P.op("vector", b_delta, reads=["sp1", "sp2", "PA"], writes=["PA", "delta"])

    def b_absA(s_):
        return s_.activation(SCR[32:48, 0:2048], SCR2[0:16, 0:2048], AF.Abs)
    P.op("scalar", b_absA, reads=["x96"], writes=["SCR", "Apre"])
    def b_Amul(v):
        return v.tensor_mul(SCR[32:48, 0:2048], SCR[32:48, 0:2048],
                            PA[64:80, 0:2048])
    P.op("vector", b_Amul, reads=["Apre", "delta"], writes=["Apre"])
    def b_Aexp(s_):
        return s_.activation(SCR[32:48, 0:2048], SCR[32:48, 0:2048],
                             AF.Exp, scale=-1.0)
    P.op("scalar", b_Aexp, reads=["Apre"], writes=["Apre"])

    def b_xc16(v):
        return v.tensor_copy(PA[96:112, 0:2048], FB[0:16, 0:2048])
    P.op("vector", b_xc16, reads=["xcbf0", "PA"], writes=["PA", "xc16"])
    def b_bx1(v):
        return v.tensor_mul(SCR2[32:48, 0:2048], SCR2[32:48, 0:2048],
                            PA[64:80, 0:2048])
    P.op("vector", b_bx1, reads=["x96", "delta"], writes=["x96", "Bx"])
    def b_bx2(v):
        return v.tensor_mul(SCR2[32:48, 0:2048], SCR2[32:48, 0:2048],
                            PA[96:112, 0:2048])
    P.op("vector", b_bx2, reads=["Bx", "xc16"], writes=["Bx"])

    def b_scan(v):
        return v.tensor_tensor_scan(PA[0:16, 0:2048], SCR[32:48, 0:2048],
                                    SCR2[32:48, 0:2048], 0.0,
                                    ALU.mult, ALU.add)
    P.op("vector", b_scan, reads=["Apre", "Bx", "sp2", "PA"],
         writes=["PA", "hscan"])

    win_dma(16)
    for half in range(2):
        for c2 in range(2):
            for dt in range(ND):
                def b_z(t, half=half, c2=c2, dt=dt):
                    return t.matmul(
                        PB[0:16, c2 * 512:(c2 + 1) * 512],
                        WBUF[:, dt * 16:(dt + 1) * 16],
                        BFX[:, dt * S + half * 1024 + c2 * 512:
                            dt * S + half * 1024 + (c2 + 1) * 512],
                        start=(dt == 0), stop=(dt == ND - 1))
                P.op("tensor", b_z, reads=["win0", "BFX"], writes=["PB"],
                     pe_chain=True, inc=(dt == ND - 1))
        def b_ze(v, half=half):
            return v.tensor_scalar_add(
                SCR2[64:80, half * 1024:(half + 1) * 1024],
                PB[0:16, 0:1024], VECS[0:16, 17:18])
        P.op("vector", b_ze, reads=["PB", "VECSbin", "x96"],
             writes=["z16", "x96"])
    def b_y16(v):
        return v.tensor_mul(SCR[0:16, 0:2048], PA[0:16, 0:2048],
                            SCR2[64:80, 0:2048])
    P.op("vector", b_y16, reads=["hscan", "z16", "delta", "SCR"],
         writes=["SCR", "y16"])
    def b_y16b(v):
        return v.tensor_copy(WBUF[0:16, 4608:6656], SCR[0:16, 0:2048])
    P.op("vector", b_y16b, reads=["y16"], writes=["y16bf"])

    if "dbg_y16" in tap_aps:
        P.dma("gpsimd", tap_aps["dbg_y16"], SCR[:, 0:2048],
              reads=["y16", "Apre", "sp1"])
    if "dbg_z16" in tap_aps:
        P.dma("gpsimd", tap_aps["dbg_z16"], SCR2[:, 0:2048],
              reads=["z16", "x96", "Bx"])
    for dt in range(ND):
        for c4 in range(4):
            def b_wo(t, dt=dt, c4=c4):
                return t.matmul(PA[:, c4 * 512:(c4 + 1) * 512],
                                WBUF[0:16, 3584 + dt * 128:3584 + (dt + 1) * 128],
                                WBUF[0:16, 4608 + c4 * 512:4608 + (c4 + 1) * 512],
                                start=True, stop=True)
            P.op("tensor", b_wo, reads=["wo16", "y16bf"], writes=["PA"],
                 pe_chain=True, inc=(c4 == 3))
        def b_res3(v, dt=dt):
            return v.tensor_add(RES[:, dt * S:(dt + 1) * S],
                                RES[:, dt * S:(dt + 1) * S], PA[:, 0:2048])
        P.op("vector", b_res3, reads=["RES", "PA"], writes=["RES"])

    _ln_feature_major(nc, P, sbufs, EPS_LN, with_mean=True)

    if upto == 3:
        flat = out.rearrange("(dt p two) d -> dt p (two d)", p=128, two=2)
        for dt in range(ND):
            P.dma("sync", flat[dt], RES[:, dt * S:(dt + 1) * S], reads=["RES"])
        return

    P.barrier()
    for dt in range(ND):
        P.dma("scalar", VECS[:, 99 + dt:100 + dt],
              dram["l1f_g1"][dt * 128:(dt + 1) * 128, 0:1], writes=["VECSg1"])
        P.dma("scalar", VECS[:, 107 + dt:108 + dt],
              dram["l1f_g2"][dt * 128:(dt + 1) * 128, 0:1], writes=["VECSg2"])
    P.dma("scalar", VECS[0:8, 115:116], dram["l1f_ebias"], writes=["VECSeb"])

    P.dma("sync", dram["x3scr"], RES[:, :], reads=["RES"], writes=["x3scr"])

    _ln_feature_major(nc, P, sbufs, EPS_RMS, with_mean=False, normalize=False)
    for dt in range(ND):
        def b_nx(v, dt=dt):
            return v.scalar_tensor_tensor(BFX[:, dt * S:(dt + 1) * S],
                                          RES[:, dt * S:(dt + 1) * S],
                                          VECS[:, 99 + dt:100 + dt],
                                          SCR[:, 0:2048], ALU.mult, ALU.mult)
        P.op("vector", b_nx, reads=["RES", "VECSg1", "rstdbc"], writes=["BFX"])

    def wslot(pos):
        return pos % 2
    def wchunk_dma(pos, src_ap, width, key, nslots=2):
        slot = pos % nslots
        P.dma("gpsimd",
              WBUF[:, slot * 2048:slot * 2048 + 8 * width].rearrange(
                  "p (t c) -> p t c", c=width),
              src_ap, reads=[f"w{slot}"], writes=[f"w{slot}", key])
        return slot

    wq_src = dram["l1f_WqT"].rearrange("(t p) c -> p t c", p=128)
    wkc_src = dram["l1f_WkcT"].rearrange("(t p) c -> p t c", p=128)
    wvc_src = dram["l1f_WvcT"].rearrange("(t p) c -> p t c", p=128)
    wo_src = dram["l1f_WoT"].rearrange("(t p) c -> p t c", p=128)
    pos = [0]
    def stream_proj(src, nt_out, fb0, key):
        for i in range(nt_out):
            slot = wchunk_dma(pos[0], src[:, :, i * 128:(i + 1) * 128], 128, key)
            pos[0] += 1
            for c4 in range(4):
                for dt in range(ND):
                    def b_p(t, slot=slot, c4=c4, dt=dt):
                        return t.matmul(
                            PA[:, c4 * 512:(c4 + 1) * 512],
                            WBUF[:, slot * 2048 + dt * 128:slot * 2048 + (dt + 1) * 128],
                            BFX[:, dt * S + c4 * 512:dt * S + (c4 + 1) * 512],
                            start=(dt == 0), stop=(dt == ND - 1))
                    P.op("tensor", b_p, reads=[f"w{slot}", key, "BFX"],
                         writes=["PA"], pe_chain=True, inc=(dt == ND - 1))
            for c4 in range(4):
                def b_pe(v, i=i, fb0=fb0, c4=c4):
                    return v.tensor_copy(
                        FB[:, fb0 + i * 2048 + c4 * 512:
                           fb0 + i * 2048 + (c4 + 1) * 512],
                        PA[:, c4 * 512:(c4 + 1) * 512])
                P.op("vector", b_pe, reads=[f"psb{c4}"], writes=[f"fb{fb0}_{i}"])
    for i in range(8):
        P.join(f"fb0_{i}", ["wd", "xcbf%d" % i] + [f"xcbf{j}" for j in range(16)])
    for i in range(2):
        P.join(f"fb16384_{i}", [f"xcbf{8 + i}"])
        P.join(f"fb20480_{i}", [f"xcbf{10 + i}"])
    for i in range(2):
        P.join(f"fb24576_{i}", [f"xcbf{12 + i}", f"xcbf{13 + i}", f"xcbf{14 + i}"])
    stream_proj(wq_src, 8, 0, "wq")
    stream_proj(wkc_src, 2, 16384, "wkc")
    stream_proj(wvc_src, 2, 20480, "wvc")

    nvc = 0
    for tt in range(16):
        for vt in range(2):
            pcc = 0 if nvc % 2 == 0 else 1024
            def b_vt(t, vt=vt, tt=tt, pcc=pcc):
                return t.transpose(PC[:, pcc:pcc + 128],
                                   FB[:, 20480 + vt * 2048 + tt * 128:
                                      20480 + vt * 2048 + (tt + 1) * 128],
                                   IDENT)
            P.op("tensor", b_vt, reads=[f"fb20480_{vt}", "CONST"],
                 writes=[f"pcv{pcc}"], pe_chain=False)
            def b_vc(v, vt=vt, tt=tt, pcc=pcc):
                return v.tensor_copy(
                    FB[:, 24576 + tt * 256 + vt * 128:
                       24576 + tt * 256 + (vt + 1) * 128],
                    PC[:, pcc:pcc + 128])
            P.op("vector", b_vc, reads=[f"pcv{pcc}"], writes=["vctok"])
            nvc += 1

    NEG0, NEG1, RS_, RCP_ = 0, 1, 2, 3
    steps = [(h, st) for h in range(4) for st in range(NT)]
    def emit_scores(k):
        h, st = steps[k]
        for tt2 in range(2):
            for c4 in range(4):
                def b_sc(t, h=h, st=st, tt2=tt2, c4=c4):
                    return t.matmul(
                        PA[:, c4 * 512:(c4 + 1) * 512],
                        FB[:, (2 * h + tt2) * 2048 + st * 128:
                           (2 * h + tt2) * 2048 + (st + 1) * 128],
                        FB[:, 16384 + tt2 * 2048 + c4 * 512:
                           16384 + tt2 * 2048 + (c4 + 1) * 512],
                        start=(tt2 == 0), stop=(tt2 == 1))
                P.op("tensor", b_sc,
                     reads=[f"fb0_{2 * h + tt2}", "fb16384_0", "fb16384_1"],
                     writes=["PA"], pe_chain=True,
                     inc=(tt2 == 1 and c4 == 3))
    emit_scores(0)
    for k in range(len(steps)):
        h, st = steps[k]
        def b_mx(v):
            return v.reduce_max(SMALL[:, NEG0:NEG0 + 1], PA[:, 0:2048],
                                axis=AXX)
        P.op("vector", b_mx, reads=["PA"], writes=["SMALL"])
        def b_ng(v):
            return v.tensor_scalar_mul(SMALL[:, NEG1:NEG1 + 1],
                                       SMALL[:, NEG0:NEG0 + 1], -1.0 / 16.0)
        P.op("vector", b_ng, reads=["SMALL"], writes=["SMALL"])
        def b_ex(s_):
            return s_.activation(SCR2[:, 0:2048], PA[:, 0:2048], AF.Exp,
                                 bias=SMALL[:, NEG1:NEG1 + 1],
                                 scale=1.0 / 16.0,
                                 accum_out=SMALL[:, RS_:RS_ + 1])
        P.op("scalar", b_ex, reads=["PA", "SMALL"], writes=["SCR2", "SMALL"])
        def b_ext(s_):
            return s_.copy(SMALL[:, RS_:RS_ + 1], SMALL[:, RS_:RS_ + 1])
        P.op("scalar", b_ext, reads=["SMALL"], writes=["SMALL"])
        if k + 1 < len(steps):
            emit_scores(k + 1)
        def b_rc4(v):
            return v.reciprocal(SMALL[:, RCP_:RCP_ + 1], SMALL[:, RS_:RS_ + 1])
        P.op("vector", b_rc4, reads=["SMALL"], writes=["SMALL"])
        def b_an(v):
            return v.tensor_scalar_mul(WBUF[:, 4096:6144], SCR2[:, 0:2048],
                                       SMALL[:, RCP_:RCP_ + 1])
        P.op("vector", b_an, reads=["SCR2", "SMALL"], writes=["anorm"])
        for tt in range(16):
            def b_tA(t, tt=tt):
                return t.transpose(PC[:, tt * 128:(tt + 1) * 128],
                                   WBUF[:, 4096 + tt * 128:4096 + (tt + 1) * 128],
                                   IDENT)
            P.op("tensor", b_tA, reads=["anorm", "CONST"], writes=["PC"],
                 pe_chain=True, inc=(tt == 15))
        def b_cA(v):
            return v.tensor_copy(WBUF[:, 6144:8192], PC[:, 0:2048])
        P.op("vector", b_cA, reads=["PC"], writes=["atr"])
        for j in range(2):
            for tt in range(16):
                def b_pv(t, j=j, tt=tt):
                    return t.matmul(
                        PB[:, j * 128:(j + 1) * 128],
                        FB[:, 24576 + tt * 256 + j * 128:
                           24576 + tt * 256 + (j + 1) * 128],
                        WBUF[:, 6144 + tt * 128:6144 + (tt + 1) * 128],
                        start=(tt == 0), stop=(tt == 15))
                P.op("tensor", b_pv, reads=["vctok", "atr"], writes=["PB"],
                     pe_chain=True, inc=(tt == 15 and j == 1))
        def b_pve(v, h=h, st=st):
            return v.tensor_copy(
                FB[:, 2 * h * 2048 + st * 128:2 * h * 2048 + (st + 1) * 128],
                PB[:, 0:128])
        P.op("vector", b_pve, reads=["PB"], writes=[f"fb0_{2 * h}"])
        def b_pve2(v, h=h, st=st):
            return v.tensor_copy(
                FB[:, (2 * h + 1) * 2048 + st * 128:
                   (2 * h + 1) * 2048 + (st + 1) * 128],
                PB[:, 128:256])
        P.op("vector", b_pve2, reads=["PB"], writes=[f"fb0_{2 * h + 1}"])

    for dt in range(ND):
        slot = wchunk_dma(pos[0], wo_src[:, :, dt * 128:(dt + 1) * 128], 128,
                          "wo")
        pos[0] += 1
        for c4 in range(4):
            for k in range(ND):
                def b_woM(t, slot=slot, c4=c4, k=k):
                    return t.matmul(
                        PA[:, c4 * 512:(c4 + 1) * 512],
                        WBUF[:, slot * 2048 + k * 128:slot * 2048 + (k + 1) * 128],
                        FB[:, k * 2048 + c4 * 512:k * 2048 + (c4 + 1) * 512],
                        start=(k == 0), stop=(k == ND - 1))
                P.op("tensor", b_woM,
                     reads=[f"w{slot}", "wo"] + [f"fb0_{k}" for k in range(8)],
                     writes=["PA"], pe_chain=True, inc=(k == ND - 1))
        def b_woE(v, dt=dt):
            return v.tensor_add(RES[:, dt * S:(dt + 1) * S],
                                RES[:, dt * S:(dt + 1) * S], PA[:, 0:2048])
        P.op("vector", b_woE, reads=["RES", "PA"], writes=["RES"])

    if "dbg_x4a" in tap_aps:
        P.dma("gpsimd", tap_aps["dbg_x4a"], RES[:, 0:2048], reads=["RES"])
    if "dbg_x4b" in tap_aps:
        P.dma("gpsimd", tap_aps["dbg_x4b"], RES[:, 5 * 2048:6 * 2048],
              reads=["RES"])
    _ln_feature_major(nc, P, sbufs, EPS_RMS, with_mean=False, normalize=False)
    for dt in range(ND):
        def b_xf(v, dt=dt):
            return v.scalar_tensor_tensor(BFX[:, dt * S:(dt + 1) * S],
                                          RES[:, dt * S:(dt + 1) * S],
                                          VECS[:, 107 + dt:108 + dt],
                                          SCR[:, 0:2048], ALU.mult, ALU.mult)
        P.op("vector", b_xf, reads=["RES", "VECSg2", "rstdbc"], writes=["BFX"])

    P.dma("sync", GWT[:, 0:64].rearrange("p (t e) -> p t e", e=8),
          dram["l1f_WrT"].rearrange("(t p) e -> p t e", p=128),
          reads=["GWT"], writes=["GWT"])
    for dt in range(ND):
        def b_wr(v, dt=dt):
            return v.tensor_scalar_mul(GWT[:, dt * 8:(dt + 1) * 8],
                                       GWT[:, dt * 8:(dt + 1) * 8],
                                       VECS[:, 107 + dt:108 + dt])
        P.op("vector", b_wr, reads=["GWT", "VECSg2"], writes=["GWT"])
    for dt in range(ND):
        for c4 in range(4):
            def b_lg2(t, dt=dt, c4=c4):
                return t.matmul(PA[0:8, c4 * 512:(c4 + 1) * 512],
                                GWT[:, dt * 8:(dt + 1) * 8],
                                RES[:, dt * S + c4 * 512:dt * S + (c4 + 1) * 512],
                                start=(dt == 0), stop=(dt == ND - 1))
            P.op("tensor", b_lg2, reads=["GWT", "RES"], writes=["PA"],
                 pe_chain=True, inc=(dt == ND - 1 and c4 == 3))
    def b_lsc(v):
        return v.tensor_mul(SCR[0:8, 0:2048], PA[0:8, 0:2048], SCR[0:8, 0:2048])
    P.op("vector", b_lsc, reads=["PA", "rstdbc", "SCR"], writes=["SCR", "lgT"])
    def b_lbi(v):
        return v.tensor_scalar_add(SCR[0:8, 0:2048], SCR[0:8, 0:2048],
                                   VECS[0:8, 115:116])
    P.op("vector", b_lbi, reads=["lgT", "VECSeb", "SCR"], writes=["SCR", "lgT"])
    for it in range(NT):
        def b_lt2(t, it=it):
            return t.transpose(PB[:, it * 8:(it + 1) * 8],
                               SCR[0:8, it * 128:(it + 1) * 128], IDF[0:8, 0:8])
        P.op("tensor", b_lt2, reads=["lgT", "SCR", "CONST2"], writes=["PB"],
             pe_chain=True, inc=(it == NT - 1))
    def b_lte2(v):
        return v.tensor_copy(SCR2[:, 0:128], PB[:, 0:128])
    P.op("vector", b_lte2, reads=["PB"], writes=["ltok"])
    _top2(nc, P, sbufs, "ltok", mode="soft")
    for it in range(NT):
        def b_cT2(t, it=it):
            return t.transpose(PA[0:8, it * 128:(it + 1) * 128],
                               SCR2[:, 608 + it * 8:616 + it * 8], IDF)
        P.op("tensor", b_cT2, reads=["comb", "CONST2"], writes=["PA"],
             pe_chain=True, inc=(it == NT - 1))
    def b_cTe2(v):
        return v.tensor_copy(SCR2[0:8, 0:2048], PA[0:8, 0:2048])
    P.op("vector", b_cTe2, reads=["PA"],
         writes=["combT", "ltok", "comb", "t2scr"])

    ew1_src = dram["l1f_eW1T"].rearrange("e (t p) f -> e p t f", p=128)
    ew3_src = dram["l1f_eW3T"].rearrange("e (t p) f -> e p t f", p=128)
    ew2_src = dram["l1f_eW2T"].rearrange("e (f p) d -> e p f d", p=128)
    sw1_src = dram["l1f_sW1T"].rearrange("(t p) f -> p t f", p=128)
    sw3_src = dram["l1f_sW3T"].rearrange("(t p) f -> p t f", p=128)
    sw2_src = dram["l1f_sW2T"].rearrange("(f p) d -> p f d", p=128)

    P.join("w2", ["anorm"])
    P.join("w3", ["atr"])
    P.join("vctok", ["vctok"])
    P.join("ew2", ["wd"] + [f"xcbf{j}" for j in range(16)] +
           [f"fb0_{i}" for i in range(8)] +
           ["fb16384_0", "fb16384_1", "fb20480_0", "fb20480_1", "vctok"])
    for htl in range(8):
        P.join(f"l1hh{htl}", ["ew2"])

    def emit_l1_ffn_expert(e):
        if e < 0:
            w1s, w3s, w2s = sw1_src, sw3_src, sw2_src
        else:
            w1s, w3s, w2s = ew1_src[e], ew3_src[e], ew2_src[e]
        P.dma("gpsimd", FB[:, 0:8192].rearrange("p (f d) -> p f d", d=1024),
              w2s, reads=["ew2"], writes=["ew2"])
        if e >= 0:
            P.dma("sync", SCR[0:1, 0:2048], SCR2[e:e + 1, 0:2048],
                  reads=["combT"], writes=["combrow", "SCR", "combbc"])
            for half in range(2):
                for c2 in range(2):
                    def b_cb2(t, half=half, c2=c2):
                        return t.matmul(PB[:, c2 * 512:(c2 + 1) * 512], ONESR_F,
                                        SCR[0:1, half * 1024 + c2 * 512:
                                            half * 1024 + (c2 + 1) * 512],
                                        start=True, stop=True)
                    P.op("tensor", b_cb2, reads=["combrow", "CONST2"],
                         writes=["PB"], pe_chain=True, inc=(c2 == 1))
                def b_cbe2(v, half=half):
                    return v.tensor_copy(SCR[:, half * 1024:(half + 1) * 1024],
                                         PB[:, 0:1024])
                P.op("vector", b_cbe2, reads=["PB"], writes=["SCR", "combbc"])
        base = pos[0]
        def load_pair(ci):
            wchunk_dma(pos[0], w1s[:, :, ci * 256:(ci + 1) * 256], 256,
                       "l1w1", nslots=4)
            pos[0] += 1
            wchunk_dma(pos[0], w3s[:, :, ci * 256:(ci + 1) * 256], 256,
                       "l1w3", nslots=4)
            pos[0] += 1
        load_pair(0)
        for ht in range(8):
            ci = ht // 2
            if ht % 2 == 0 and ci + 1 < 4:
                load_pair(ci + 1)
            s1 = (base + 2 * ci) % 4
            s3 = (base + 2 * ci + 1) % 4
            hofs = (ht % 2) * 128
            for tbb in range(2):
                for c2 in range(2):
                    for dt in range(ND):
                        def b_G(t, s1=s1, hofs=hofs, tbb=tbb, c2=c2, dt=dt):
                            return t.matmul(
                                PA[:, c2 * 512:(c2 + 1) * 512],
                                WBUF[:, s1 * 2048 + dt * 256 + hofs:
                                     s1 * 2048 + dt * 256 + hofs + 128],
                                BFX[:, dt * S + tbb * 1024 + c2 * 512:
                                    dt * S + tbb * 1024 + (c2 + 1) * 512],
                                start=(dt == 0), stop=(dt == ND - 1))
                        P.op("tensor", b_G, reads=[f"w{s1}", "l1w1", "BFX"],
                             writes=["PAg1"], pe_chain=True, inc=(dt == ND - 1))
                for c2 in range(2):
                    for dt in range(ND):
                        def b_U(t, s3=s3, hofs=hofs, tbb=tbb, c2=c2, dt=dt):
                            return t.matmul(
                                PA[:, 1024 + c2 * 512:1024 + (c2 + 1) * 512],
                                WBUF[:, s3 * 2048 + dt * 256 + hofs:
                                     s3 * 2048 + dt * 256 + hofs + 128],
                                BFX[:, dt * S + tbb * 1024 + c2 * 512:
                                    dt * S + tbb * 1024 + (c2 + 1) * 512],
                                start=(dt == 0), stop=(dt == ND - 1))
                        P.op("tensor", b_U, reads=[f"w{s3}", "l1w3", "BFX"],
                             writes=["PAu1"], pe_chain=True, inc=(dt == ND - 1))
                hcol = 8192 + ht * 2048 + tbb * 1024
                def b_sl(s_, hcol=hcol):
                    return s_.activation(FB[:, hcol:hcol + 1024],
                                         PA[:, 0:1024], AF.Silu)
                P.op("scalar", b_sl, reads=["PAg1"],
                     writes=[f"l1hh{ht}"])
                def b_hm(v, hcol=hcol):
                    return v.tensor_mul(FB[:, hcol:hcol + 1024],
                                        FB[:, hcol:hcol + 1024],
                                        PA[:, 1024:2048])
                P.op("vector", b_hm, reads=[f"l1hh{ht}", "PAu1"],
                     writes=[f"l1hh{ht}"])
        for dt in range(ND):
            for c4 in range(4):
                for ht in range(8):
                    def b_eo2(t, dt=dt, c4=c4, ht=ht):
                        return t.matmul(
                            PA[:, c4 * 512:(c4 + 1) * 512],
                            FB[:, ht * 1024 + dt * 128:ht * 1024 + (dt + 1) * 128],
                            FB[:, 8192 + ht * 2048 + c4 * 512:
                               8192 + ht * 2048 + (c4 + 1) * 512],
                            start=(ht == 0), stop=(ht == 7))
                    P.op("tensor", b_eo2,
                         reads=["ew2", f"l1hh{ht}"], writes=["PA"],
                         pe_chain=True, inc=(ht == 7))
            if e < 0:
                def b_acc2(v, dt=dt):
                    return v.tensor_add(RES[:, dt * S:(dt + 1) * S],
                                        RES[:, dt * S:(dt + 1) * S],
                                        PA[:, 0:2048])
                P.op("vector", b_acc2, reads=["RES", "PA"], writes=["RES"])
            else:
                for c4 in range(4):
                    def b_cm2(v, c4=c4):
                        return v.tensor_mul(SMALL[:, 0:512],
                                            PA[:, c4 * 512:(c4 + 1) * 512],
                                            SCR[:, c4 * 512:(c4 + 1) * 512])
                    P.op("vector", b_cm2, reads=["PA", "combbc"],
                         writes=["SMALL"])
                    def b_ac2(v, dt=dt, c4=c4):
                        return v.tensor_add(
                            RES[:, dt * S + c4 * 512:dt * S + (c4 + 1) * 512],
                            RES[:, dt * S + c4 * 512:dt * S + (c4 + 1) * 512],
                            SMALL[:, 0:512])
                    P.op("vector", b_ac2, reads=["RES", "SMALL"],
                         writes=["RES"])

    if not _DBG.get("skip_l1shared"):
        emit_l1_ffn_expert(-1)
    if not _DBG.get("skip_l1moe"):
        for e in range(_DBG.get("n_l1_experts", E0)):
            emit_l1_ffn_expert(e)

    if "dbg_x5a" in tap_aps:
        P.dma("gpsimd", tap_aps["dbg_x5a"], RES[:, 0:2048], reads=["RES"])
    if "dbg_l1comb" in tap_aps:
        P.dma("gpsimd", tap_aps["dbg_l1comb"], SCR2[:, 0:2048],
              reads=["combT"])
    for dt in range(ND):
        P.dma("sync", SCR[:, 0:2048],
              dram["x3scr"][:, dt * S:(dt + 1) * S],
              reads=["x3scr"], writes=["SCR", "x3back", "combbc", "combrow"])
        def b_x3a(v, dt=dt):
            return v.tensor_add(RES[:, dt * S:(dt + 1) * S],
                                RES[:, dt * S:(dt + 1) * S], SCR[:, 0:2048])
        P.op("vector", b_x3a, reads=["RES", "SCR", "x3back"], writes=["RES"])
    _ln_feature_major(nc, P, sbufs, EPS_LN, with_mean=True, normalize=True)
    P.barrier()
    _flip_res(nc, P, RES, PA, IDF, tok_to_feat=False)
    op_t = out.rearrange("(t p) d -> t p d", p=128)
    for it in range(NT):
        P.dma("sync", op_t[it], RES[:, it * D:(it + 1) * D], reads=["RES"])
    return


def _flip_res(nc, P, RES, PA, IDF, tok_to_feat=True):
    if tok_to_feat:
        def perm(k):
            dt, it = divmod(k, 16)
            return it * 8 + dt
    else:
        def perm(k):
            it, dt = divmod(k, 8)
            return dt * 16 + it
    P.fork("RES", [f"RESc{i}" for i in range(128)])
    seen = set()
    bank = [0]
    def tr(dstbank, src_chunk):
        def b(t, dstbank=dstbank, src_chunk=src_chunk):
            return t.transpose(PA[:, dstbank * 512:dstbank * 512 + 128],
                               RES[:, src_chunk * 128:(src_chunk + 1) * 128],
                               IDF)
        P.op("tensor", b, reads=[f"RESc{src_chunk}", "CONST2"],
             writes=[f"paB{dstbank}"])
    def cp(dstchunk, srcbank):
        def b(v, dstchunk=dstchunk, srcbank=srcbank):
            return v.tensor_copy(RES[:, dstchunk * 128:(dstchunk + 1) * 128],
                                 PA[:, srcbank * 512:srcbank * 512 + 128])
        P.op("vector", b, reads=[f"paB{srcbank}"], writes=[f"RESc{dstchunk}"])
    for k0 in range(128):
        if k0 in seen:
            continue
        cyc = [k0]
        k = perm(k0)
        while k != k0:
            cyc.append(k)
            k = perm(k)
        seen.update(cyc)
        if len(cyc) == 1:
            b = bank[0]; bank[0] = (bank[0] + 1) % 3
            tr(b, k0)
            cp(k0, b)
            continue
        tr(3, cyc[0])
        for i in range(len(cyc) - 1):
            b = bank[0]; bank[0] = (bank[0] + 1) % 3
            tr(b, cyc[i + 1])
            cp(cyc[i], b)
        cp(cyc[-1], 3)
    P.join("RES", [f"RESc{i}" for i in range(128)])


def _top2(nc, P, sbufs, ltok_key, mode="renorm"):
    SCR2 = sbufs["SCR2"]; SMALL = sbufs["SMALL"]
    if mode == "soft":
        def b_m1s(v):
            return v.reduce_max(SCR2[:, 128:144],
                                SCR2[:, 0:128].rearrange("p (t e) -> p t e", e=8),
                                axis=AXX)
        P.op("vector", b_m1s, reads=[ltok_key], writes=["t2m1", "t2scr"])
        def b_negm(v):
            return v.tensor_scalar_mul(SCR2[:, 144:160], SCR2[:, 128:144], -1.0)
        P.op("vector", b_negm, reads=["t2m1"], writes=["t2negm"])
        for it in range(NT):
            def b_es(s_, it=it):
                return s_.activation(SCR2[:, it * 8:(it + 1) * 8],
                                     SCR2[:, it * 8:(it + 1) * 8], AF.Exp,
                                     bias=SCR2[:, 144 + it:145 + it],
                                     accum_out=SCR2[:, 160 + it:161 + it])
            P.op("scalar", b_es, reads=[ltok_key, "t2negm"],
                 writes=[ltok_key, "t2z"])
            def b_est(s_, it=it):
                return s_.copy(SCR2[:, 160 + it:161 + it],
                               SCR2[:, 160 + it:161 + it])
            P.op("scalar", b_est, reads=["t2z"], writes=["t2z"])
        def b_rz(v):
            return v.reciprocal(SCR2[:, 176:192], SCR2[:, 160:176])
        P.op("vector", b_rz, reads=["t2z"], writes=["t2rz"])
        for it in range(NT):
            def b_pn(v, it=it):
                return v.tensor_scalar_mul(SCR2[:, it * 8:(it + 1) * 8],
                                           SCR2[:, it * 8:(it + 1) * 8],
                                           SCR2[:, 176 + it:177 + it])
            P.op("vector", b_pn, reads=[ltok_key, "t2rz"], writes=[ltok_key])
    def b_m1(v):
        return v.reduce_max(SCR2[:, 128:144],
                            SCR2[:, 0:128].rearrange("p (t e) -> p t e", e=8),
                            axis=AXX)
    P.op("vector", b_m1, reads=[ltok_key], writes=["t2m1", "t2scr"])
    for it in range(NT):
        def b_eq1(v, it=it):
            return v.tensor_scalar(SCR2[:, 224 + it * 8:232 + it * 8],
                                   SCR2[:, it * 8:(it + 1) * 8],
                                   SCR2[:, 128 + it:129 + it], None,
                                   ALU.is_ge)
        P.op("vector", b_eq1, reads=[ltok_key, "t2m1"], writes=["t2eq1"])
        def b_l2(v, it=it):
            return v.scalar_tensor_tensor(SCR2[:, 352 + it * 8:360 + it * 8],
                                          SCR2[:, 224 + it * 8:232 + it * 8],
                                          -1e30,
                                          SCR2[:, it * 8:(it + 1) * 8],
                                          ALU.mult, ALU.add)
        P.op("vector", b_l2, reads=["t2eq1", ltok_key], writes=["t2l2"])
    def b_m2(v):
        return v.reduce_max(SCR2[:, 144:160],
                            SCR2[:, 352:480].rearrange("p (t e) -> p t e", e=8),
                            axis=AXX)
    P.op("vector", b_m2, reads=["t2l2"], writes=["t2m2"])
    def b_dm(v):
        return v.tensor_sub(SCR2[:, 160:176], SCR2[:, 144:160],
                            SCR2[:, 128:144])
    P.op("vector", b_dm, reads=["t2m1", "t2m2"], writes=["t2dm"])
    if mode == "soft":
        def b_sg(s_):
            return s_.activation(SMALL[:, 32:48], SCR2[:, 160:176], AF.Sigmoid)
        P.op("scalar", b_sg, reads=["t2dm"], writes=["SMALL"])
        def b_w1s(v):
            return v.tensor_scalar(SMALL[:, 16:32], SMALL[:, 32:48], -1.0, 1.0,
                                   ALU.mult, ALU.add)
        P.op("vector", b_w1s, reads=["SMALL"], writes=["SMALL"])
    else:
        def b_ed(s_):
            return s_.activation(SCR2[:, 160:176], SCR2[:, 160:176], AF.Exp)
        P.op("scalar", b_ed, reads=["t2dm"], writes=["t2dm"])
        def b_w1a(v):
            return v.tensor_scalar_add(SMALL[:, 0:16], SCR2[:, 160:176], 1.0)
        P.op("vector", b_w1a, reads=["t2dm"], writes=["SMALL"])
        def b_w1(v):
            return v.reciprocal(SMALL[:, 16:32], SMALL[:, 0:16])
        P.op("vector", b_w1, reads=["SMALL"], writes=["SMALL"])
        def b_w2(v):
            return v.tensor_mul(SMALL[:, 32:48], SCR2[:, 160:176],
                                SMALL[:, 16:32])
        P.op("vector", b_w2, reads=["SMALL", "t2dm"], writes=["SMALL"])
    for it in range(NT):
        def b_eq2(v, it=it):
            return v.tensor_scalar(SCR2[:, 480 + it * 8:488 + it * 8],
                                   SCR2[:, 352 + it * 8:360 + it * 8],
                                   SCR2[:, 144 + it:145 + it], None,
                                   ALU.is_ge)
        P.op("vector", b_eq2, reads=["t2l2", "t2m2"], writes=["t2eq2"])
        def b_e2w(v, it=it):
            return v.tensor_scalar_mul(SCR2[:, 480 + it * 8:488 + it * 8],
                                       SCR2[:, 480 + it * 8:488 + it * 8],
                                       SMALL[:, 32 + it:33 + it])
        P.op("vector", b_e2w, reads=["t2eq2", "SMALL"], writes=["t2eq2"])
        def b_comb(v, it=it):
            return v.scalar_tensor_tensor(SCR2[:, 608 + it * 8:616 + it * 8],
                                          SCR2[:, 224 + it * 8:232 + it * 8],
                                          SMALL[:, 16 + it:17 + it],
                                          SCR2[:, 480 + it * 8:488 + it * 8],
                                          ALU.mult, ALU.add)
        P.op("vector", b_comb, reads=["t2eq1", "t2eq2", "SMALL"],
             writes=["comb"])


def _ln_feature_major(nc, P, sbufs, eps, with_mean=True, normalize=True):
    RES = sbufs["RES"]; SCR = sbufs["SCR"]; SCR2 = sbufs["SCR2"]
    PA = sbufs["PA"]; PB = sbufs["PB"]
    CONST2 = sbufs["CONST2"]
    ONESC_F = CONST2[:, 128:129]
    ONESR_F = CONST2[0:1, 144:272]

    if with_mean:
        for dt in range(ND):
            for c4 in range(4):
                def b_sy(t, dt=dt, c4=c4):
                    return t.matmul(PA[0:1, c4 * 512:(c4 + 1) * 512], ONESC_F,
                                    RES[:, dt * S + c4 * 512:dt * S + (c4 + 1) * 512],
                                    start=(dt == 0), stop=(dt == ND - 1))
                P.op("tensor", b_sy, reads=["RES", "CONST2"], writes=["PA"],
                     pe_chain=True, inc=(dt == ND - 1 and c4 == 3))
        def b_mur(v):
            return v.tensor_scalar_mul(SCR2[0:1, 0:2048], PA[0:1, 0:2048],
                                       1.0 / D)
        P.op("vector", b_mur, reads=["PA"],
             writes=["murow", "combT", "SCR2a"])
    for dt in range(ND):
        def b_sq(s_, dt=dt):
            return s_.activation(SCR[:, 0:2048],
                                 RES[:, dt * S:(dt + 1) * S], AF.Square)
        P.op("scalar", b_sq, reads=["RES"], writes=["SCR"])
        for c4 in range(4):
            def b_ss(t, dt=dt, c4=c4):
                return t.matmul(PA[0:1, c4 * 512:(c4 + 1) * 512], ONESC_F,
                                SCR[:, c4 * 512:(c4 + 1) * 512],
                                start=(dt == 0), stop=(dt == ND - 1))
            P.op("tensor", b_ss, reads=["SCR", "CONST2"], writes=["PA"],
                 pe_chain=True, inc=(c4 == 3))
    def b_vt(v):
        return v.tensor_scalar(SCR[0:1, 0:2048], PA[0:1, 0:2048],
                               1.0 / D, eps, ALU.mult, ALU.add)
    P.op("vector", b_vt, reads=["PA"], writes=["SCR", "vrow"])
    if with_mean:
        def b_msq(v):
            return v.tensor_mul(PA[0:1, 0:2048], SCR2[0:1, 0:2048],
                                SCR2[0:1, 0:2048])
        P.op("vector", b_msq, reads=["murow"], writes=["PA"])
        def b_v2(v):
            return v.tensor_sub(SCR[0:1, 0:2048], SCR[0:1, 0:2048],
                                PA[0:1, 0:2048])
        P.op("vector", b_v2, reads=["vrow", "PA"], writes=["SCR", "vrow"])
    def b_sqr(s_):
        return s_.activation(SCR[0:1, 0:2048], SCR[0:1, 0:2048], AF.Sqrt)
    P.op("scalar", b_sqr, reads=["vrow"], writes=["vrow"])
    def b_rc(v):
        return v.reciprocal(SCR[0:1, 0:2048], SCR[0:1, 0:2048])
    P.op("vector", b_rc, reads=["vrow"], writes=["vrow"])

    def bcast(row_ap, row_key, dst, dst_keys):
        for half in range(2):
            for c2 in range(2):
                def b_b(t, half=half, c2=c2):
                    return t.matmul(PB[:, c2 * 512:(c2 + 1) * 512], ONESR_F,
                                    row_ap[0:1, half * 1024 + c2 * 512:
                                           half * 1024 + (c2 + 1) * 512],
                                    start=True, stop=True)
                P.op("tensor", b_b, reads=[row_key, "CONST2"], writes=["PB"],
                     pe_chain=True, inc=(c2 == 1))
            def b_be(v, half=half, dst=dst):
                return v.tensor_copy(dst[:, half * 1024:(half + 1) * 1024],
                                     PB[:, 0:1024])
            P.op("vector", b_be, reads=["PB"], writes=list(dst_keys))
    bcast(SCR, "vrow", SCR, ["SCR", "rstdbc"])
    if with_mean:
        bcast(SCR2, "murow", SCR2, ["SCR2a", "murow", "mubc"])
    if not normalize:
        return
    for dt in range(ND):
        if with_mean:
            def b_n1(v, dt=dt):
                return v.tensor_sub(RES[:, dt * S:(dt + 1) * S],
                                    RES[:, dt * S:(dt + 1) * S],
                                    SCR2[:, 0:2048])
            P.op("vector", b_n1, reads=["RES", "mubc"], writes=["RES"])
        def b_n2(v, dt=dt):
            return v.tensor_mul(RES[:, dt * S:(dt + 1) * S],
                                RES[:, dt * S:(dt + 1) * S],
                                SCR[:, 0:2048])
        P.op("vector", b_n2, reads=["RES", "rstdbc"], writes=["RES"])


def _ln_token_major(P, RES, SCR, SCR2, SU0, SQ0, MU0, RS0, eps):
    for it in range(NT):
        def b_su(v, it=it):
            return v.reduce_sum(SCR2[:, SU0 + it:SU0 + it + 1],
                                RES[:, it * D:(it + 1) * D], axis=AXX)
        P.op("vector", b_su, reads=["RES"], writes=["lnsu"])
        def b_sq2(s, it=it):
            return s.activation(SCR[:, 0:1024], RES[:, it * D:(it + 1) * D],
                                AF.Square,
                                accum_out=SCR2[:, SQ0 + it:SQ0 + it + 1])
        P.op("scalar", b_sq2, reads=["RES"], writes=["SCR", "lnsq"])
        def b_sq2t(s_, it=it):
            return s_.copy(SCR2[:, SQ0 + it:SQ0 + it + 1],
                           SCR2[:, SQ0 + it:SQ0 + it + 1])
        P.op("scalar", b_sq2t, reads=["lnsq"], writes=["lnsq"])
    def b_mu(v):
        return v.tensor_scalar_mul(SCR2[:, MU0:MU0 + 16], SCR2[:, SU0:SU0 + 16],
                                   1.0 / D)
    P.op("vector", b_mu, reads=["lnsu"], writes=["lnmu"])
    def b_var(v):
        return v.tensor_scalar(SCR2[:, RS0:RS0 + 16], SCR2[:, SQ0:SQ0 + 16],
                               1.0 / D, eps, ALU.mult, ALU.add)
    P.op("vector", b_var, reads=["lnsq"], writes=["lnrs"])
    def b_musq(v):
        return v.tensor_mul(SCR[:, 0:16], SCR2[:, MU0:MU0 + 16],
                            SCR2[:, MU0:MU0 + 16])
    P.op("vector", b_musq, reads=["lnmu"], writes=["SCR"])
    def b_var3(v):
        return v.tensor_sub(SCR2[:, RS0:RS0 + 16], SCR2[:, RS0:RS0 + 16],
                            SCR[:, 0:16])
    P.op("vector", b_var3, reads=["SCR", "lnrs"], writes=["lnrs"])
    def b_sqrt(s):
        return s.activation(SCR2[:, RS0:RS0 + 16], SCR2[:, RS0:RS0 + 16], AF.Sqrt)
    P.op("scalar", b_sqrt, reads=["lnrs"], writes=["lnrs"])
    def b_rc2(v):
        return v.reciprocal(SCR2[:, RS0:RS0 + 16], SCR2[:, RS0:RS0 + 16])
    P.op("vector", b_rc2, reads=["lnrs"], writes=["lnrs"])
    for it in range(NT):
        def b_n(v, it=it):
            return v.tensor_scalar(RES[:, it * D:(it + 1) * D],
                                   RES[:, it * D:(it + 1) * D],
                                   SCR2[:, MU0 + it:MU0 + it + 1],
                                   SCR2[:, RS0 + it:RS0 + it + 1],
                                   ALU.subtract, ALU.mult)
        P.op("vector", b_n, reads=["RES", "lnmu", "lnrs"], writes=["RES"])


def _prep_inputs(inputs):
    T = lambda a: np.ascontiguousarray(np.swapaxes(np.asarray(a), -1, -2))
    d = {}
    d["l0_gate_wT"] = T(inputs["L0_ffn_gate_w"])
    d["l0_gate_b"] = np.asarray(inputs["L0_ffn_gate_b"]).reshape(E0, 1)
    d["l0_wgT"] = T(inputs["L0_ffn_gate_proj"])
    d["l0_wuT"] = T(inputs["L0_ffn_up_proj"])
    d["l0_wdT"] = T(inputs["L0_ffn_down_proj"])
    w_in = np.asarray(inputs["L1_attn_w_in"])
    d["l1_winT"] = w_in[0:2064].T
    d["l1_bin"] = np.asarray(inputs["L1_attn_b_in"])[0:2064].reshape(2064, 1)
    d["l1_wconv"] = np.asarray(inputs["L1_attn_w_conv"]).reshape(2048, 4)
    d["l1_bconv"] = np.asarray(inputs["L1_attn_b_conv"]).reshape(2048, 1)
    wx = np.asarray(inputs["L1_attn_w_xproj"])
    wdt = np.asarray(inputs["L1_attn_w_dt"])
    x96 = np.zeros((96, 2048), np.float32)
    x96[0:16] = wx[0:16]
    x96[32:48] = wx[16:32]
    x96[64:80] = wdt
    d["l1_wx96T"] = x96.T
    d["l1_bdt"] = np.asarray(inputs["L1_attn_b_dt"]).reshape(16, 1)
    d["l1_wout16T"] = np.asarray(inputs["L1_attn_w_out"])[:, 0:16].T
    d["l1f_g1"] = np.asarray(inputs["L1_ffn_g1"]).reshape(D, 1)
    d["l1f_g2"] = np.asarray(inputs["L1_ffn_g2"]).reshape(D, 1)
    d["l1f_WqT"] = T(inputs["L1_ffn_Wq"])
    d["l1f_WkcT"] = T(inputs["L1_ffn_Wkc"])
    d["l1f_WvcT"] = T(inputs["L1_ffn_Wvc"])
    d["l1f_WoT"] = T(inputs["L1_ffn_Wo"])
    d["l1f_WrT"] = T(inputs["L1_ffn_Wr"])
    d["l1f_ebias"] = np.asarray(inputs["L1_ffn_expert_bias"]).reshape(E0, 1)
    d["l1f_sW1T"] = T(inputs["L1_ffn_sW1"])
    d["l1f_sW2T"] = T(inputs["L1_ffn_sW2"])
    d["l1f_sW3T"] = T(inputs["L1_ffn_sW3"])
    d["l1f_eW1T"] = T(inputs["L1_ffn_eW1"])
    d["l1f_eW2T"] = T(inputs["L1_ffn_eW2"])
    d["l1f_eW3T"] = T(inputs["L1_ffn_eW3"])
    return {k: np.ascontiguousarray(v, dtype=np.float32) for k, v in d.items()}


def kernel(**inputs):
    x = np.ascontiguousarray(np.asarray(inputs["L0_attn_x"], dtype=np.float32))
    weights = _prep_inputs(inputs)
    nc = build_program(upto=4)
    in_maps = []
    for c in range(8):
        m = {"xr": np.ascontiguousarray(np.roll(x, -c, axis=0))}
        m.update(weights)
        in_maps.append(m)
    res = run_bass_kernel_spmd(nc, in_maps, core_ids=list(range(8)))
    return np.stack([res.results[c]["out"] for c in range(8)], axis=0)
